# version 04
# baseline (speedup 1.0000x reference)
import numpy as np
import ml_dtypes
from contextlib import ExitStack

import concourse.bass as bass
import concourse.mybir as mybir
from concourse.bass_utils import run_bass_kernel_spmd

F32 = mybir.dt.float32
BF16 = mybir.dt.bfloat16
I32 = mybir.dt.int32
AF = mybir.ActivationFunctionType
ALU = mybir.AluOpType
AX = mybir.AxisListType
bfloat16 = ml_dtypes.bfloat16

D = 2048
S = 2048
NH = 16
HD = 128
IFF = 8192
NCORE = 8
NSEQ = 2
DC = D // 128
ICN = IFF // 128
TCN = 4
NT128 = 16
SM_SCALE = 1.0 / float(np.sqrt(HD))
LN_EPS = 1e-5
POOLN = 24

TAPS = ()
RUN_SEQS = list(range(NSEQ))
RUN_PHASES = ("A", "B", "D", "C")

ENGINES = ("sync", "gpsimd", "vector", "scalar", "tensor")


class Sem:
    __slots__ = ("h", "n")

    def __init__(self, handle):
        self.h = handle
        self.n = 0


class Builder:
    def __init__(self, nc, es):
        self.nc = nc
        self.streams = {e: [] for e in ENGINES}
        self.c = {e: Sem(es.enter_context(nc.semaphore(f"c_{e}"))) for e in ENGINES}
        self.pool = {q: [Sem(es.enter_context(nc.semaphore(f"p_{q}{i}")))
                         for i in range(POOLN)]
                     for q in ("sync", "gpsimd", "scalar")}
        self.pidx = {"sync": 0, "gpsimd": 0, "scalar": 0}

    def wait(self, eng, ctr, val):
        if val > 0:
            self.streams[eng].append(["wait", ctr.h, val])

    def waitt(self, eng, tok):
        if not tok:
            return
        if isinstance(tok, list):
            for t in tok:
                self.waitt(eng, t)
            return
        self.streams[eng].append(["wait", tok[0].h, tok[1]])

    def _emit(self, eng, rec, inc):
        ctr = self.c[eng]
        if inc:
            ctr.n += inc
            rec.append((ctr.h, inc))
        else:
            rec.append(None)
        self.streams[eng].append(rec)
        return ctr.n

    def dma(self, eng, out, in_, transpose=False):
        pool = self.pool[eng]
        s = pool[self.pidx[eng] % POOLN]
        self.pidx[eng] += 1
        if s.n > 0:
            self.streams[eng].append(["wait", s.h, s.n])
        s.n += 16
        self.streams[eng].append(["dma", out, in_, transpose, (s.h, 16)])
        return (s, s.n)

    def mm(self, out, lhsT, rhs, start, stop, inc=0):
        return self._emit("tensor", ["mm", out, lhsT, rhs, start, stop], inc)

    def act(self, out, in_, func, bias=None, scale=None, inc=1):
        return self._emit("scalar", ["act", out, in_, func, bias, scale], inc)

    def copy(self, out, in_, inc=1, eng="vector"):
        return self._emit(eng, ["copy", out, in_], inc)

    def tt(self, out, a, b, op, inc=1, eng="vector"):
        return self._emit(eng, ["tt", out, a, b, op], inc)

    def ts(self, out, in_, s1, s2, op0, op1, inc=1, eng="vector"):
        return self._emit(eng, ["ts", out, in_, s1, s2, op0, op1], inc)

    def stt(self, out, in0, sc, in1, op0, op1, inc=1, eng="vector"):
        return self._emit(eng, ["stt", out, in0, sc, in1, op0, op1], inc)

    def red(self, out, in_, inc=1, eng="vector"):
        return self._emit(eng, ["red", out, in_], inc)

    def recip(self, out, in_, inc=1):
        return self._emit("vector", ["recip", out, in_], inc)

    def snap(self):
        return {e: self.c[e].n for e in ("tensor", "vector", "scalar")}

    def ensure_tail_inc(self, eng):
        for rec in reversed(self.streams[eng]):
            if rec[0] == "wait":
                continue
            if rec[0] == "dma":
                return
            if rec[-1] is None:
                ctr = self.c[eng]
                ctr.n += 1
                rec[-1] = (ctr.h, 1)
            return

    def barrier(self):
        for e in ENGINES:
            self.ensure_tail_inc(e)
        cvals = {e: self.c[e].n for e in ENGINES}
        pvals = [(s, s.n) for q in self.pool.values() for s in q if s.n > 0]
        for e in ENGINES:
            for o in ENGINES:
                if o != e and cvals[o] > 0:
                    self.wait(e, self.c[o], cvals[o])
            for s, v in pvals:
                self.streams[e].append(["wait", s.h, v])


def _replay(eng, ops):
    for r in ops:
        k = r[0]
        if k == "wait":
            eng.wait_ge(r[1], r[2])
            continue
        if k == "dma":
            ins = eng.dma_start(out=r[1], in_=r[2], transpose=r[3])
        elif k == "mm":
            ins = eng.matmul(r[1], r[2], r[3], start=r[4], stop=r[5])
        elif k == "act":
            kw = {}
            if r[4] is not None:
                kw["bias"] = r[4]
            if r[5] is not None:
                kw["scale"] = r[5]
            ins = eng.activation(r[1], r[2], r[3], **kw)
        elif k == "copy":
            ins = eng.tensor_copy(r[1], r[2])
        elif k == "tt":
            ins = eng.tensor_tensor(r[1], r[2], r[3], r[4])
        elif k == "ts":
            ins = eng.tensor_scalar(r[1], r[2], r[3], r[4], r[5], r[6])
        elif k == "stt":
            ins = eng.scalar_tensor_tensor(r[1], r[2], r[3], r[4], r[5], r[6])
        elif k == "red":
            ins = eng.reduce_sum(r[1], r[2], axis=AX.X)
        elif k == "recip":
            ins = eng.reciprocal(r[1], r[2])
        else:
            raise ValueError(k)
        if r[-1] is not None:
            ins.then_inc(r[-1][0], r[-1][1])


def build_nc():
    nc = bass.Bass("TRN2", target_bir_lowering=False)

    x_in = nc.dram_tensor("x", [NSEQ, S, D], F32, kind="ExternalInput")
    wp_in = nc.dram_tensor("wp", [3 * D, D], F32, kind="ExternalInput")
    wo_in = nc.dram_tensor("wo", [D, D], F32, kind="ExternalInput")
    wg_in = nc.dram_tensor("wg", [IFF, D], I32, kind="ExternalInput")
    wu_in = nc.dram_tensor("wu", [IFF, D], I32, kind="ExternalInput")
    wd_in = nc.dram_tensor("wd", [D, IFF], I32, kind="ExternalInput")
    gs_in = nc.dram_tensor("gs", [128, ICN], F32, kind="ExternalInput")
    us_in = nc.dram_tensor("us", [128, ICN], F32, kind="ExternalInput")
    dsb_in = nc.dram_tensor("dsb", [128, D], F32, kind="ExternalInput")
    mask_in = nc.dram_tensor("mask", [128, 128], F32, kind="ExternalInput")
    smsc_in = nc.dram_tensor("smsc", [128, 1], F32, kind="ExternalInput")
    onesc_in = nc.dram_tensor("onesc", [128, 1], BF16, kind="ExternalInput")
    onesr_in = nc.dram_tensor("onesr", [1, 128], F32, kind="ExternalInput")
    y_out = nc.dram_tensor("y", [NSEQ, S, D], F32, kind="ExternalOutput")

    wp_bf = nc.dram_tensor("wp_bf", [3 * D, D], BF16, kind="Internal")
    wo_bf = nc.dram_tensor("wo_bf", [D, D], BF16, kind="Internal")
    wg_bf = nc.dram_tensor("wg_bf", [IFF, D], BF16, kind="Internal")
    wu_bf = nc.dram_tensor("wu_bf", [IFF, D], BF16, kind="Internal")
    wd_bf = nc.dram_tensor("wd_bf", [D, IFF], BF16, kind="Internal")
    cwp = nc.dram_tensor("cwp", [DC, 128, 3 * D], BF16, kind="Internal")
    cwo = nc.dram_tensor("cwo", [DC, 128, D], BF16, kind="Internal")
    cwg = nc.dram_tensor("cwg", [DC, 128, IFF], BF16, kind="Internal")
    cwu = nc.dram_tensor("cwu", [DC, 128, IFF], BF16, kind="Internal")
    cwd = nc.dram_tensor("cwd", [ICN, 128, D], BF16, kind="Internal")
    xnd = nc.dram_tensor("xnd", [S, D], BF16, kind="Internal")
    vd = nc.dram_tensor("vd", [DC, 128, HD], BF16, kind="Internal")
    attd = nc.dram_tensor("attd", [2, DC, 512, 128], BF16, kind="Internal")
    aod = nc.dram_tensor("aod", [NH, HD, S], BF16, kind="Internal")
    gd = nc.dram_tensor("gd", [S, D], BF16, kind="Internal")

    taps = {}
    for t in TAPS:
        if t == "lnstats":
            taps[t] = nc.dram_tensor("tap_lnstats", [NT128, 4, 128], F32,
                                     kind="ExternalOutput")
            continue
        shp = {"xn": [S, D], "g": [S, D], "ao": [NH, HD, S]}[t.rstrip("01")]
        taps[t] = nc.dram_tensor("tap_" + t, shp, BF16, kind="ExternalOutput")

    A0 = 18432
    my_sbuf = set()

    def sb(name, shape, dtype, off):
        my_sbuf.add(name)
        return nc.alloc_sbuf_tensor_at(name, shape, dtype, offset=off)

    xnT = sb("xnT", [128, DC * S], BF16, A0)
    h_all = sb("h_all", [128, ICN * 512], BF16, A0 + 65536)
    st = A0 + 131072
    gs_sb = sb("gs_sb", [128, ICN], F32, st); st += 256
    us_sb = sb("us_sb", [128, ICN], F32, st); st += 256
    mask_sb = sb("mask_sb", [128, 128], F32, st); st += 512
    dsb_sb = sb("dsb_sb", [128, D], BF16, st); st += 4096
    onesc_sb = sb("onesc_sb", [128, 1], BF16, st); st += 32
    onesr_sb = sb("onesr_sb", [1, 128], F32, st); st += 512
    sm_t = sb("sm_t", [128, 1], F32, st); st += 32
    nmean_t = sb("nmean_t", [128, 1], F32, st); st += 32
    ssq_t = sb("ssq_t", [128, 1], F32, st); st += 32
    var_t = sb("var_t", [128, 1], F32, st); st += 32
    lnv_t = sb("lnv_t", [128, 1], F32, st); st += 32
    rstd_t = sb("rstd_t", [128, 1], F32, st); st += 32
    smsc_sb = sb("smsc_sb", [128, 1], F32, st); st += 32
    st = (st + 127) & ~127

    S1 = st
    xa_tile = sb("xa_tile", [128, D], F32, S1)
    xn_tile = sb("xn_tile", [128, D], BF16, S1 + 8192)
    q_sb = sb("q_sb", [128, S], BF16, S1)
    k_sb = sb("k_sb", [128, S], BF16, S1 + 4096)
    vtm_sb = sb("vtm_sb", [128, DC * HD], BF16, S1 + 8192)
    att_sb = sb("att_sb", [128, S], BF16, S1 + 12288)
    attT_sb = sb("attT_sb", [128, 4096], BF16, S1 + 16384)
    v_sb = sb("v_sb", [128, S], BF16, S1 + 16384)
    wq_st = sb("wq_st", [128, DC * 128], BF16, S1 + 24576)
    wk_st = sb("wk_st", [128, DC * 128], BF16, S1 + 28672)
    wv_st = sb("wv_st", [128, DC * 128], BF16, S1 + 32768)
    ao_st0 = sb("ao_st0", [128, 512], BF16, S1 + 36864)
    ao_st1 = sb("ao_st1", [128, 512], BF16, S1 + 37888)
    S1_END = S1 + 38912

    S2 = S1_END
    zr_sb = sb("zr_sb", [1, 512], F32, S2)
    zbc_sb = sb("zbc_sb", [128, 512], F32, S2 + 2048)
    bt0 = sb("bt0", [128, 4096], BF16, S2 + 4096)
    bt1 = sb("bt1", [128, 4096], BF16, S2 + 12288)
    wg_st = [sb(f"wg_st{i}", [128, DC * 128], BF16, S2 + 4096 * i) for i in range(2)]
    wu_st = [sb(f"wu_st{i}", [128, DC * 128], BF16, S2 + 8192 + 4096 * i) for i in range(2)]
    wd_st = [sb(f"wd_st{i}", [128, 2048], BF16, S2 + 16384 + 4096 * i) for i in range(2)]
    silu_tmp = sb("silu_tmp", [128, 512], BF16, S2 + 24576)
    dtmp = sb("dtmp", [128, 512], F32, S2 + 25600)
    g_st = [sb(f"g_st{i}", [128, 1024], BF16, S2 + 27648 + 2048 * i) for i in range(2)]
    woq_st = sb("woq_st", [128, DC * 512], BF16, S2)
    ao_ld = [sb(f"ao_ld{i}", [128, DC * 128], BF16, S2 + 16384 + 4096 * i) for i in range(2)]
    xc_ld = [sb(f"xc_ld{i}", [128, 512], F32, S2 + 24576 + 2048 * i) for i in range(2)]
    gc_ld = [sb(f"gc_ld{i}", [128, 512], BF16, S2 + 28672 + 1024 * i) for i in range(2)]
    cmb = [sb(f"cmb{i}", [128, 512], F32, S2 + 30720 + 2048 * i) for i in range(2)]
    S2_END = S2 + 34816
    assert S2_END <= 229344, f"SBUF overflow: {S2_END}"

    es = ExitStack()
    with es:
        psumA = es.enter_context(nc.psum_tensor("psumA", [128, 2048], F32))
        psumB = es.enter_context(nc.psum_tensor("psumB", [128, 2048], F32))
        b = Builder(nc, es)

        ct = {}
        ct["gs"] = b.dma("sync", gs_sb[:], gs_in.ap())
        ct["us"] = b.dma("sync", us_sb[:], us_in.ap())
        ct["mask"] = b.dma("sync", mask_sb[:], mask_in.ap())
        ct["smsc"] = b.dma("sync", smsc_sb[:], smsc_in.ap())
        ct["onesc"] = b.dma("sync", onesc_sb[:], onesc_in.ap())
        ct["onesr"] = b.dma("sync", onesr_sb[:], onesr_in.ap())
        ct["dsb"] = b.dma("gpsimd", dsb_sb[:], dsb_in.ap())

        cast_toks = {}
        for nm, src, dst, rows in (("wp", wp_in, wp_bf, 3 * D), ("wo", wo_in, wo_bf, D),
                                   ("wg", wg_in, wg_bf, IFF), ("wu", wu_in, wu_bf, IFF),
                                   ("wd", wd_in, wd_bf, D)):
            toks = []
            for ch in range(rows // 2048):
                toks.append(b.dma("gpsimd", dst[ch * 2048:(ch + 1) * 2048, :],
                                  src[ch * 2048:(ch + 1) * 2048, :]))
            cast_toks[nm] = toks

        bt_free = [0, 0]
        cache_toks = {"wp": [], "wo": [], "wg": [], "wu": [], "wd": []}

        def build_block(nm, src_bf, cache, j, hf, rows, minus128):
            bid = hf
            buf = (bt0, bt1)[bid]
            half = rows // 2
            if j == 0:
                b.waitt("sync", cast_toks[nm])
            b.waitt("sync", bt_free[bid])
            vt = b.dma("sync", buf[:, 0:half],
                       src_bf[hf * half:(hf + 1) * half, j * 128:(j + 1) * 128],
                       transpose=True)
            if minus128:
                b.waitt("vector", vt)
                vv = b.ts(buf[:, 0:half], buf[:, 0:half], -128.0, None,
                          ALU.add, ALU.bypass)
                b.wait("gpsimd", b.c["vector"], vv)
            else:
                b.waitt("gpsimd", vt)
            tok = b.dma("gpsimd", cache[j, :, hf * half:(hf + 1) * half], buf[:, 0:half])
            bt_free[bid] = tok
            cache_toks[nm].append(tok)

        def build_all(nm, src_bf, cache, nblk, rows, minus128):
            for j in range(nblk):
                for hf in range(2):
                    build_block(nm, src_bf, cache, j, hf, rows, minus128)

        ffn_build_jobs = []
        if set(RUN_PHASES) & {"B", "D", "C"}:
            for nm, src_bf, cache, nblk, rows, m128 in (
                    ("wg", wg_bf, cwg, DC, IFF, True),
                    ("wu", wu_bf, cwu, DC, IFF, True),
                    ("wd", wd_bf, cwd, ICN, D, True)):
                for j in range(nblk):
                    for hf in range(2):
                        ffn_build_jobs.append(
                            (lambda nm=nm, s_=src_bf, c_=cache, j=j, hf=hf,
                                    r=rows, m=m128: build_block(nm, s_, c_, j, hf, r, m)))

        def phase_A(s, G, t_xnT_prev):
            x_s = x_in[s]
            b.wait("sync", b.c["tensor"], G.get("B_t", 0))
            b.wait("sync", b.c["vector"], G.get("B_v", 0))
            b.wait("sync", b.c["scalar"], G.get("B_s", 0))
            store_toks = []
            v_sq = v_ap = 0
            for t in range(NT128):
                b.wait("sync", b.c["scalar"], v_sq)
                b.wait("sync", b.c["vector"], v_ap)
                if store_toks:
                    b.waitt("sync", store_toks[-1])
                vx = b.dma("sync", xa_tile[:], x_s[t * 128:(t + 1) * 128, :])
                b.waitt("vector", vx)
                v1 = b.red(sm_t[:], xa_tile[:])
                b.wait("vector", b.c["vector"], v1)
                v2 = b.ts(nmean_t[:], sm_t[:], -1.0 / D, None, ALU.mult, ALU.bypass)
                b.wait("scalar", b.c["vector"], v2)
                if store_toks:
                    b.waitt("scalar", store_toks[-1])
                v_sq = b.act(xn_tile[:], xa_tile[:], AF.Square, bias=nmean_t[:])
                b.wait("vector", b.c["scalar"], v_sq)
                v_red = b.red(ssq_t[:], xn_tile[:])
                b.wait("vector", b.c["vector"], v_red)
                v3 = b.ts(var_t[:], ssq_t[:], 1.0 / D, LN_EPS, ALU.mult, ALU.add)
                b.wait("scalar", b.c["vector"], v3)
                v4 = b.act(lnv_t[:], var_t[:], AF.Ln)
                b.wait("scalar", b.c["scalar"], v4)
                v5 = b.act(rstd_t[:], lnv_t[:], AF.Exp, scale=-0.5)
                b.wait("vector", b.c["scalar"], v5)
                v_ap = b.ts(xn_tile[:], xa_tile[:], nmean_t[:], rstd_t[:],
                            ALU.add, ALU.mult)
                b.wait("gpsimd", b.c["vector"], v_ap)
                if t == 0:
                    b.waitt("gpsimd", t_xnT_prev)
                store_toks.append(b.dma("gpsimd", xnd[t * 128:(t + 1) * 128, :],
                                        xn_tile[:]))
                if "lnstats" in taps and s == RUN_SEQS[-1]:
                    for i, tile in enumerate((nmean_t, lnv_t, var_t, rstd_t)):
                        store_toks.append(
                            b.dma("gpsimd", taps["lnstats"][t, i, :], tile[:, 0]))
            b.waitt("sync", store_toks)
            b.wait("sync", b.c["tensor"], G.get("D_t", 0))
            t_xnT = []
            for dc in range(DC):
                t_xnT.append(b.dma("sync", xnT[:, dc * S:(dc + 1) * S],
                                   xnd[:, dc * 128:(dc + 1) * 128], transpose=True))
            if f"xn{s}" in taps:
                b.waitt("gpsimd", store_toks)
                b.dma("gpsimd", taps[f"xn{s}"].ap(), xnd.ap())
            return t_xnT

        BST = {"t_vtm": 0, "t_att_store": 0, "t_attd_tls": [[], []],
               "t_ao": [0, 0]}

        def phase_B(s, G, t_xnT, t_aold_prev, background):
            v_wst = G.get("B_t", 0)
            t_vtm = BST["t_vtm"]
            t_att_store = BST["t_att_store"]
            t_attd_tls = BST["t_attd_tls"]
            v_attT = G.get("C_t", 0)
            t_ao = BST["t_ao"]
            v_aoev = G.get("C_v", 0)
            v_exp_last = 0
            v_exp_slot = [0, 0]
            t_aod_all = []
            first_xnT = [True]
            if s == 0:
                b.waitt("scalar", ct["smsc"])
                b.waitt("tensor", [ct["onesc"], ct["onesr"]])
                b.waitt("vector", ct["mask"])
                b.waitt("sync", cache_toks["wp"])
            for h in range(NH):
                if background:
                    for _ in range(14):
                        if ffn_build_jobs:
                            ffn_build_jobs.pop(0)()
                b.wait("sync", b.c["tensor"], v_wst)
                t_wld = []
                for st_buf, part in ((wq_st, 0), (wk_st, 1), (wv_st, 2)):
                    off = part * D + h * HD
                    t_wld.append(b.dma(
                        "sync", st_buf[:].rearrange("p (c i) -> p c i", c=DC),
                        cwp[:, :, off:off + HD].rearrange("c p i -> p c i")))

                def proj(st_buf, psum, evac_to, wv, wsc):
                    b.waitt("tensor", t_wld)
                    if first_xnT[0]:
                        b.waitt("tensor", t_xnT)
                        first_xnT[0] = False
                    b.wait("tensor", b.c["vector"], wv)
                    b.wait("tensor", b.c["scalar"], wsc)
                    vmm = 0
                    for dc in range(DC):
                        lhsT = st_buf[:, dc * 128:(dc + 1) * 128]
                        for n in range(4):
                            vmm = b.mm(psum[:, n * 512:(n + 1) * 512], lhsT,
                                       xnT[:, dc * S + n * 512:dc * S + (n + 1) * 512],
                                       start=(dc == 0), stop=(dc == DC - 1),
                                       inc=(1 if (dc == DC - 1 and n == 3) else 0))
                    b.wait("vector", b.c["tensor"], vmm)
                    return b.copy(evac_to, psum[:]), vmm

                v_q_evac, _ = proj(wq_st, psumA[:], q_sb[:], v_aoev, 0)
                v_k_evac, _ = proj(wk_st, psumB[:], k_sb[:], v_aoev, v_exp_last)
                b.wait("vector", b.c["tensor"], v_attT)
                v_v_evac, vv_mm = proj(wv_st, psumA[:], v_sb[:], v_q_evac, 0)
                v_wst = vv_mm
                b.wait("gpsimd", b.c["vector"], v_v_evac)
                b.waitt("gpsimd", t_vtm)
                t_vst = b.dma("gpsimd", vd.rearrange("c h t -> h c t"),
                              v_sb[:].rearrange("p (c t) -> p c t", c=DC))
                b.waitt("sync", t_vst)
                b.wait("sync", b.c["tensor"], v_attT)
                t_vtm = b.dma("sync", vtm_sb[:], vd.rearrange("c h t -> (c h) t"),
                              transpose=True)

                for qg in range(4):
                    abuf = qg % 2
                    nch = 4 * (qg + 1)
                    t_att_group = []
                    for qs in range(4):
                        qglob = qg * 4 + qs
                        ktlen = (qglob + 1) * 128
                        q_lhsT = q_sb[:, qglob * 128:(qglob + 1) * 128]
                        nsc = (ktlen + 511) // 512
                        for j in range(nsc):
                            w0 = j * 512
                            w1 = min(ktlen, w0 + 512)
                            slot = (j % 2) * 512
                            b.wait("tensor", b.c["scalar"], v_exp_slot[j % 2])
                            if qg == 0 and qs == 0 and j == 0:
                                b.wait("tensor", b.c["vector"], v_v_evac)
                            vsc = b.mm(psumB[:, slot:slot + (w1 - w0)], q_lhsT,
                                       k_sb[:, w0:w1], start=True, stop=True, inc=1)
                            if w1 == ktlen:
                                b.wait("vector", b.c["tensor"], vsc)
                                vmk = b.tt(
                                    psumB[:, slot + (w1 - w0) - 128:slot + (w1 - w0)],
                                    psumB[:, slot + (w1 - w0) - 128:slot + (w1 - w0)],
                                    mask_sb[:], ALU.add)
                                b.wait("scalar", b.c["vector"], vmk)
                            else:
                                b.wait("scalar", b.c["tensor"], vsc)
                            if j == 0:
                                b.waitt("scalar", t_att_store)
                            v_exp_slot[j % 2] = b.act(att_sb[:, w0:w1],
                                                      psumB[:, slot:slot + (w1 - w0)],
                                                      AF.Exp, scale=smsc_sb[:])
                        v_exp_last = max(v_exp_slot)
                        b.wait("gpsimd", b.c["scalar"], v_exp_last)
                        if qs == 0:
                            b.waitt("gpsimd", t_attd_tls[abuf])
                            t_attd_tls[abuf] = []
                        nco = (ktlen + 127) // 128
                        t_att_store = b.dma(
                            "gpsimd",
                            attd[abuf].rearrange("c q k -> q c k")[qs * 128:(qs + 1) * 128, 0:nco, :],
                            att_sb[:, 0:ktlen].rearrange("p (c k) -> p c k", c=nco))
                        t_att_group.append(t_att_store)
                    zps = psumB[0:1, 1536:2048]
                    zbc_ps = psumB[:, 1024:1536]
                    nhalf = (nch + 7) // 8
                    for hf in range(nhalf):
                        c0 = hf * 8
                        c1 = min(nch, c0 + 8)
                        b.waitt("sync", t_att_group)
                        b.wait("sync", b.c["tensor"], v_attT)
                        t_tl = b.dma("sync", attT_sb[:, 0:(c1 - c0) * 512],
                                     attd[abuf].rearrange("c q k -> (c q) k")[c0 * 512:c1 * 512, :],
                                     transpose=True)
                        t_attd_tls[abuf].append(t_tl)
                        b.waitt("tensor", t_tl)
                        if hf == 0:
                            b.waitt("tensor", t_vtm)
                        for c in range(c0, c1):
                            qlo = max(0, (c - 4 * qg) * 128)
                            if qlo >= 512:
                                continue
                            rhs = attT_sb[:, (c - c0) * 512 + qlo:(c - c0 + 1) * 512]
                            b.mm(zps[:, qlo:512], onesc_sb[:], rhs,
                                 start=(c == 0), stop=(c == nch - 1))
                            b.mm(psumA[:, qg * 512 + qlo:(qg + 1) * 512],
                                 vtm_sb[:, c * 128:(c + 1) * 128], rhs,
                                 start=(c == 0), stop=(c == nch - 1), inc=1)
                        v_attT = b.c["tensor"].n
                    b.wait("vector", b.c["tensor"], v_attT)
                    vrec = b.recip(zr_sb[:], zps)
                    b.wait("tensor", b.c["vector"], vrec)
                    vbc = b.mm(zbc_ps, onesr_sb[:], zr_sb[:], start=True, stop=True,
                               inc=1)
                    b.wait("vector", b.c["tensor"], vbc)
                    vcp = b.copy(zbc_sb[:], zbc_ps)
                    ao_buf = (ao_st0, ao_st1)[qg % 2]
                    b.wait("vector", b.c["vector"], vcp)
                    b.waitt("vector", t_ao[qg % 2])
                    v_aoev = b.tt(ao_buf[:], psumA[:, qg * 512:(qg + 1) * 512],
                                  zbc_sb[:], ALU.mult)
                    b.wait("gpsimd", b.c["vector"], v_aoev)
                    if h == 0 and qg == 0:
                        b.waitt("gpsimd", t_aold_prev)
                    t_ao[qg % 2] = b.dma("gpsimd", aod[h, :, qg * 512:(qg + 1) * 512],
                                         ao_buf[:])
                    t_aod_all.append(t_ao[qg % 2])
            BST["t_vtm"] = t_vtm
            BST["t_att_store"] = t_att_store
            BST["t_attd_tls"] = t_attd_tls
            BST["t_ao"] = t_ao
            if f"ao{s}" in taps:
                b.waitt("gpsimd", t_aod_all)
                b.dma("gpsimd", taps[f"ao{s}"].ap(), aod.ap())
            return t_aod_all

        def phase_D(s, G, t_gc_prev):
            t_gstore = [0, 0]
            v_w_ld = [G.get("B_t", 0)] * 2
            v_wd_ld = [G.get("B_t", 0)] * 2
            v_sil_bid = [G.get("B_s", 0)] * 2
            v_h_bid = [G.get("B_v", 0)] * 2
            v_devac = G.get("B_v", 0)
            v_dsilu = G.get("C_s", 0)
            t_gd = {}
            b.wait("sync", b.c["vector"], G.get("B_v", 0))
            if s == 0:
                b.waitt("scalar", ct["gs"])
                b.waitt("vector", [ct["us"], ct["dsb"]])
                b.waitt("sync", cache_toks["wg"] + cache_toks["wu"] + cache_toks["wd"])
            for tc in range(TCN):
                t0 = tc * 512
                for ic in range(ICN):
                    bid = ic % 2
                    b.wait("sync", b.c["tensor"], v_w_ld[bid])
                    tg = b.dma("sync",
                               wg_st[bid][:].rearrange("p (c i) -> p c i", c=DC),
                               cwg[:, :, ic * 128:(ic + 1) * 128].rearrange("c p i -> p c i"))
                    tu = b.dma("sync",
                               wu_st[bid][:].rearrange("p (c i) -> p c i", c=DC),
                               cwu[:, :, ic * 128:(ic + 1) * 128].rearrange("c p i -> p c i"))
                    gsl = bid * 512
                    usl = 1024 + bid * 512
                    b.waitt("tensor", [tg, tu])
                    b.wait("tensor", b.c["scalar"], v_sil_bid[bid])
                    b.wait("tensor", b.c["vector"], v_h_bid[bid])
                    if ic < 2:
                        b.wait("tensor", b.c["vector"], v_devac)
                    v_gmm = v_umm = 0
                    for dc in range(DC):
                        v_gmm = b.mm(psumA[:, gsl:gsl + 512],
                                     wg_st[bid][:, dc * 128:(dc + 1) * 128],
                                     xnT[:, dc * S + t0:dc * S + t0 + 512],
                                     start=(dc == 0), stop=(dc == DC - 1),
                                     inc=(1 if dc == DC - 1 else 0))
                    for dc in range(DC):
                        v_umm = b.mm(psumA[:, usl:usl + 512],
                                     wu_st[bid][:, dc * 128:(dc + 1) * 128],
                                     xnT[:, dc * S + t0:dc * S + t0 + 512],
                                     start=(dc == 0), stop=(dc == DC - 1),
                                     inc=(1 if dc == DC - 1 else 0))
                    v_w_ld[bid] = v_umm
                    b.wait("scalar", b.c["tensor"], v_gmm)
                    b.wait("scalar", b.c["vector"], v_h_bid[1 - bid])
                    v_sil_bid[bid] = b.act(silu_tmp[:], psumA[:, gsl:gsl + 512],
                                           AF.Silu, scale=gs_sb[:, ic:ic + 1])
                    b.wait("vector", b.c["scalar"], v_sil_bid[bid])
                    b.wait("vector", b.c["tensor"], v_umm)
                    v_h_bid[bid] = b.stt(h_all[:, ic * 512:(ic + 1) * 512],
                                         psumA[:, usl:usl + 512],
                                         us_sb[:, ic:ic + 1], silu_tmp[:],
                                         ALU.mult, ALU.mult)
                v_h_all = v_h_bid[(ICN - 1) % 2]
                for hh in range(2):
                    hof = hh * 1024
                    for ic in range(ICN):
                        pair, sub = ic // 2, ic % 2
                        bid = pair % 2
                        if sub == 0:
                            b.wait("sync", b.c["tensor"], v_wd_ld[bid])
                            tw = b.dma("sync",
                                       wd_st[bid][:].rearrange("p (c h) -> p c h", c=2),
                                       cwd[ic:ic + 2, :, hof:hof + 1024].rearrange("c p h -> p c h"))
                            tw_cur = tw
                            b.waitt("tensor", tw_cur)
                        if ic == 0:
                            b.wait("tensor", b.c["vector"], v_h_all)
                            b.wait("tensor", b.c["vector"], v_devac)
                        for tl in range(4):
                            ps = (psumA, psumA, psumB, psumB)[tl]
                            po = (0, 1024, 0, 1024)[tl]
                            for n in range(2):
                                b.mm(ps[:, po + n * 512:po + (n + 1) * 512],
                                     h_all[:, ic * 512 + tl * 128:ic * 512 + (tl + 1) * 128],
                                     wd_st[bid][:, sub * 1024 + n * 512:sub * 1024 + (n + 1) * 512],
                                     start=(ic == 0), stop=(ic == ICN - 1),
                                     inc=(1 if (tl == 3 and n == 1) else 0))
                        if sub == 1:
                            v_wd_ld[bid] = b.c["tensor"].n
                    v_dmm_all = b.c["tensor"].n
                    for tl in range(4):
                        ps = (psumA, psumA, psumB, psumB)[tl]
                        po = (0, 1024, 0, 1024)[tl]
                        gb = tl % 2
                        for hf2 in range(2):
                            b.wait("vector", b.c["tensor"], v_dmm_all)
                            b.wait("vector", b.c["scalar"], v_dsilu)
                            v_devac = b.tt(dtmp[:], ps[:, po + hf2 * 512:po + (hf2 + 1) * 512],
                                           dsb_sb[:, hof + hf2 * 512:hof + (hf2 + 1) * 512],
                                           ALU.mult)
                            b.wait("scalar", b.c["vector"], v_devac)
                            if hf2 == 0:
                                b.waitt("scalar", t_gstore[gb])
                            v_dsilu = b.act(g_st[gb][:, hf2 * 512:(hf2 + 1) * 512],
                                            dtmp[:], AF.Silu)
                        b.wait("gpsimd", b.c["scalar"], v_dsilu)
                        tglob = tc * 4 + tl
                        if tglob < 2 and hh == 0:
                            b.waitt("gpsimd", t_gc_prev)
                            t_gc_prev = 0
                        tok = b.dma(
                            "gpsimd",
                            gd[t0 + tl * 128:t0 + (tl + 1) * 128, hof:hof + 1024],
                            g_st[gb][:])
                        t_gstore[gb] = tok
                        t_gd.setdefault(tglob, []).append(tok)
            if f"g{s}" in taps:
                b.waitt("gpsimd", [t for t in t_gstore if t])
                b.dma("gpsimd", taps[f"g{s}"].ap(), gd.ap())
            return t_gd

        def phase_C(s, G, t_gd, t_aod_all):
            t_x = [0, 0]
            t_g = [0, 0]
            t_store = [0, 0]
            v_cmb = [G.get("D_v", 0)] * 2
            v_mm = [G.get("D_t", 0)] * 2
            t_gc_all = []
            t_aold_all = []
            b.wait("sync", b.c["scalar"], G.get("D_s", 0))
            b.wait("sync", b.c["vector"], G.get("D_v", 0))
            b.waitt("sync", t_aod_all)
            b.waitt("sync", [tk for toks in t_gd.values() for tk in toks])
            if s == 0:
                b.waitt("sync", cache_toks["wo"])
            first_cmb = [True]
            for oq in range(4):
                oof = oq * 512
                b.wait("sync", b.c["tensor"], max(v_mm))
                t_wo = b.dma("sync", woq_st[:].rearrange("p (c o) -> p c o", c=DC),
                             cwo[:, :, oof:oof + 512].rearrange("c p o -> p c o"))
                for t in range(NT128):
                    bid = t % 2
                    b.wait("sync", b.c["tensor"], v_mm[bid])
                    b.wait("sync", b.c["vector"], v_cmb[bid])
                    t_ao_ld = b.dma(
                        "sync", ao_ld[bid][:].rearrange("p (c i) -> p c i", c=DC),
                        aod[:, :, t * 128:(t + 1) * 128].rearrange("c p i -> p c i"))
                    t_aold_all.append(t_ao_ld)
                    t_x[bid] = b.dma("sync", xc_ld[bid][:],
                                     x_in[s][t * 128:(t + 1) * 128, oof:oof + 512])
                    t_g[bid] = b.dma("sync", gc_ld[bid][:],
                                     gd[t * 128:(t + 1) * 128, oof:oof + 512])
                    t_gc_all.append(t_g[bid])
                    ps = (psumA, psumB)[bid]
                    b.waitt("tensor", [t_wo, t_ao_ld])
                    b.wait("tensor", b.c["vector"], v_cmb[bid])
                    vm = 0
                    for dc in range(DC):
                        vm = b.mm(ps[:, 0:512], ao_ld[bid][:, dc * 128:(dc + 1) * 128],
                                  woq_st[:, dc * 512:(dc + 1) * 512],
                                  start=(dc == 0), stop=(dc == DC - 1),
                                  inc=(1 if dc == DC - 1 else 0))
                    v_mm[bid] = vm
                    b.wait("vector", b.c["tensor"], vm)
                    if first_cmb[0]:
                        b.wait("vector", b.c["scalar"], G.get("D_s", 0))
                        first_cmb[0] = False
                    b.waitt("vector", [t_x[bid], t_g[bid], t_store[bid]])
                    vc1 = b.stt(cmb[bid][:], ps[:, 0:512], 1.0, gc_ld[bid][:],
                                ALU.mult, ALU.mult)
                    b.wait("vector", b.c["vector"], vc1)
                    v_cmb[bid] = b.tt(cmb[bid][:], cmb[bid][:], xc_ld[bid][:], ALU.add)
                    b.wait("gpsimd", b.c["vector"], v_cmb[bid])
                    t_store[bid] = b.dma(
                        "gpsimd", y_out[s][t * 128:(t + 1) * 128, oof:oof + 512],
                        cmb[bid][:])
            return t_gc_all, t_aold_all

        G = {}
        t_xnT_prev = 0
        t_aold_prev = 0
        t_gc_prev = 0
        first = True
        for s in RUN_SEQS:
            t_xnT = phase_A(s, G, t_xnT_prev) if "A" in RUN_PHASES else []
            t_xnT_prev = t_xnT
            if first and set(RUN_PHASES) & {"B", "C"}:
                build_all("wp", wp_bf, cwp, DC, 3 * D, False)
                build_all("wo", wo_bf, cwo, DC, D, False)
            if "B" in RUN_PHASES:
                t_aod_all = phase_B(s, G, t_xnT, t_aold_prev, background=first)
                if first:
                    while ffn_build_jobs:
                        ffn_build_jobs.pop(0)()
                G["B_t"], G["B_v"], G["B_s"] = (b.c["tensor"].n, b.c["vector"].n,
                                                b.c["scalar"].n)
            else:
                t_aod_all = []
            if "D" in RUN_PHASES:
                t_gd = phase_D(s, G, t_gc_prev)
                G["D_t"], G["D_v"], G["D_s"] = (b.c["tensor"].n, b.c["vector"].n,
                                                b.c["scalar"].n)
            else:
                t_gd = {}
            if "C" in RUN_PHASES:
                t_gc_prev, t_aold_prev = phase_C(s, G, t_gd, t_aod_all)
                G["C_t"], G["C_v"], G["C_s"] = (b.c["tensor"].n, b.c["vector"].n,
                                                b.c["scalar"].n)
            first = False
        b.barrier()

        with nc.Block() as block:
            @block.sync
            def _(eng):
                _replay(eng, b.streams["sync"])

            @block.gpsimd
            def _(eng):
                _replay(eng, b.streams["gpsimd"])

            @block.vector
            def _(eng):
                _replay(eng, b.streams["vector"])

            @block.scalar
            def _(eng):
                _replay(eng, b.streams["scalar"])

            @block.tensor
            def _(eng):
                _replay(eng, b.streams["tensor"])

    for f in nc.m.functions:
        for alloc in f.allocations:
            locs = getattr(alloc, "memorylocations", None)
            if not locs:
                continue
            for loc in locs:
                if loc.type != "SB":
                    continue
                base = loc.name.split("_set")[0]
                parts = base.rsplit("_", 1)
                if len(parts) == 2 and parts[1].isdigit():
                    base = parts[0]
                if base in my_sbuf:
                    continue
                sz = 1
                for d_ in (loc.dims or [1]):
                    sz *= d_
                if loc.name.startswith("DynamicDMAScratch"):
                    continue
                assert loc.addr + sz <= A0 or loc.addr >= S2_END, \
                    f"auto SBUF alloc {loc.name} at {loc.addr} overlaps manual region"
    return nc


def kernel(L0_attn_x, L0_attn_w_project, L0_attn_w_out, L0_ffn_gate_w,
           L0_ffn_gate_scale, L0_ffn_up_w, L0_ffn_up_scale, L0_ffn_down_w,
           L0_ffn_down_scale):
    x = np.ascontiguousarray(np.asarray(L0_attn_x, dtype=np.float32))
    wp = np.ascontiguousarray(np.asarray(L0_attn_w_project, dtype=np.float32))
    wo = np.ascontiguousarray(np.asarray(L0_attn_w_out, dtype=np.float32))
    wg = np.ascontiguousarray(np.asarray(L0_ffn_gate_w, dtype=np.int32))
    wu = np.ascontiguousarray(np.asarray(L0_ffn_up_w, dtype=np.int32))
    wd = np.ascontiguousarray(np.asarray(L0_ffn_down_w, dtype=np.int32))
    gs = np.ascontiguousarray(np.asarray(L0_ffn_gate_scale, dtype=np.float32).reshape(ICN, 128).T)
    us = np.ascontiguousarray(np.asarray(L0_ffn_up_scale, dtype=np.float32).reshape(ICN, 128).T)
    ds = np.asarray(L0_ffn_down_scale, dtype=np.float32).reshape(1, -1)
    dsb = np.ascontiguousarray(np.broadcast_to(ds, (128, D)))
    mask = np.ascontiguousarray(np.triu(np.full((128, 128), -1e9, dtype=np.float32), k=1))
    smsc = np.full((128, 1), SM_SCALE, dtype=np.float32)
    onesc = np.ones((128, 1), dtype=bfloat16)
    onesr = np.ones((1, 128), dtype=np.float32)

    nc = build_nc()
    in_maps = []
    for c in range(NCORE):
        in_maps.append({"x": x[NSEQ * c:NSEQ * (c + 1)], "wp": wp, "wo": wo,
                        "wg": wg, "wu": wu, "wd": wd, "gs": gs, "us": us,
                        "dsb": dsb, "mask": mask, "smsc": smsc,
                        "onesc": onesc, "onesr": onesr})
    res = run_bass_kernel_spmd(nc, in_maps, core_ids=list(range(NCORE)))
    out = np.concatenate([r["y"] for r in res.results], axis=0)
    if TAPS:
        kernel.tap_results = [{k: v for k, v in r.items() if k.startswith("tap_")}
                              for r in res.results]
    return out


# version 05
# speedup vs baseline: 1.4445x; 1.4445x over previous
import numpy as np
import ml_dtypes
from contextlib import ExitStack

import concourse.bass as bass
import concourse.mybir as mybir
from concourse.bass_utils import run_bass_kernel_spmd

F32 = mybir.dt.float32
BF16 = mybir.dt.bfloat16
I32 = mybir.dt.int32
AF = mybir.ActivationFunctionType
ALU = mybir.AluOpType
AX = mybir.AxisListType
bfloat16 = ml_dtypes.bfloat16

D = 2048
S = 2048
NH = 16
HD = 128
IFF = 8192
NCORE = 8
NSEQ = 2
DC = D // 128
ICN = IFF // 128
TCN = 4
NT128 = 16
SM_SCALE = 1.0 / float(np.sqrt(HD))
LN_EPS = 1e-5
POOLN = 24

TAPS = ()
RUN_SEQS = list(range(NSEQ))
RUN_PHASES = ("A", "B", "D", "C")

ENGINES = ("sync", "gpsimd", "vector", "scalar", "tensor")


class Sem:
    __slots__ = ("h", "n")

    def __init__(self, handle):
        self.h = handle
        self.n = 0


class Builder:
    def __init__(self, nc, es):
        self.nc = nc
        self.streams = {e: [] for e in ENGINES}
        self.c = {e: Sem(es.enter_context(nc.semaphore(f"c_{e}"))) for e in ENGINES}
        self.pool = {q: [Sem(es.enter_context(nc.semaphore(f"p_{q}{i}")))
                         for i in range(POOLN)]
                     for q in ("sync", "gpsimd", "scalar")}
        self.pidx = {"sync": 0, "gpsimd": 0, "scalar": 0}

    def wait(self, eng, ctr, val):
        if val > 0:
            self.streams[eng].append(["wait", ctr.h, val])

    def waitt(self, eng, tok):
        if not tok:
            return
        if isinstance(tok, list):
            for t in tok:
                self.waitt(eng, t)
            return
        self.streams[eng].append(["wait", tok[0].h, tok[1]])

    def _emit(self, eng, rec, inc):
        ctr = self.c[eng]
        if inc:
            ctr.n += inc
            rec.append((ctr.h, inc))
        else:
            rec.append(None)
        self.streams[eng].append(rec)
        return ctr.n

    def dma(self, eng, out, in_, transpose=False):
        pool = self.pool[eng]
        s = pool[self.pidx[eng] % POOLN]
        self.pidx[eng] += 1
        if s.n > 0:
            self.streams[eng].append(["wait", s.h, s.n])
        s.n += 16
        self.streams[eng].append(["dma", out, in_, transpose, (s.h, 16)])
        return (s, s.n)

    def mm(self, out, lhsT, rhs, start, stop, inc=0):
        return self._emit("tensor", ["mm", out, lhsT, rhs, start, stop], inc)

    def act(self, out, in_, func, bias=None, scale=None, inc=1):
        return self._emit("scalar", ["act", out, in_, func, bias, scale], inc)

    def copy(self, out, in_, inc=1, eng="vector"):
        return self._emit(eng, ["copy", out, in_], inc)

    def tt(self, out, a, b, op, inc=1, eng="vector"):
        return self._emit(eng, ["tt", out, a, b, op], inc)

    def ts(self, out, in_, s1, s2, op0, op1, inc=1, eng="vector"):
        return self._emit(eng, ["ts", out, in_, s1, s2, op0, op1], inc)

    def stt(self, out, in0, sc, in1, op0, op1, inc=1, eng="vector"):
        return self._emit(eng, ["stt", out, in0, sc, in1, op0, op1], inc)

    def red(self, out, in_, inc=1, eng="vector"):
        return self._emit(eng, ["red", out, in_], inc)

    def recip(self, out, in_, inc=1):
        return self._emit("vector", ["recip", out, in_], inc)

    def snap(self):
        return {e: self.c[e].n for e in ("tensor", "vector", "scalar")}

    def ensure_tail_inc(self, eng):
        for rec in reversed(self.streams[eng]):
            if rec[0] == "wait":
                continue
            if rec[0] == "dma":
                return
            if rec[-1] is None:
                ctr = self.c[eng]
                ctr.n += 1
                rec[-1] = (ctr.h, 1)
            return

    def barrier(self):
        for e in ENGINES:
            self.ensure_tail_inc(e)
        cvals = {e: self.c[e].n for e in ENGINES}
        pvals = [(s, s.n) for q in self.pool.values() for s in q if s.n > 0]
        for e in ENGINES:
            for o in ENGINES:
                if o != e and cvals[o] > 0:
                    self.wait(e, self.c[o], cvals[o])
            for s, v in pvals:
                self.streams[e].append(["wait", s.h, v])


def _replay(eng, ops):
    for r in ops:
        k = r[0]
        if k == "wait":
            eng.wait_ge(r[1], r[2])
            continue
        if k == "dma":
            ins = eng.dma_start(out=r[1], in_=r[2], transpose=r[3])
        elif k == "mm":
            ins = eng.matmul(r[1], r[2], r[3], start=r[4], stop=r[5])
        elif k == "act":
            kw = {}
            if r[4] is not None:
                kw["bias"] = r[4]
            if r[5] is not None:
                kw["scale"] = r[5]
            ins = eng.activation(r[1], r[2], r[3], **kw)
        elif k == "copy":
            ins = eng.tensor_copy(r[1], r[2])
        elif k == "tt":
            ins = eng.tensor_tensor(r[1], r[2], r[3], r[4])
        elif k == "ts":
            ins = eng.tensor_scalar(r[1], r[2], r[3], r[4], r[5], r[6])
        elif k == "stt":
            ins = eng.scalar_tensor_tensor(r[1], r[2], r[3], r[4], r[5], r[6])
        elif k == "red":
            ins = eng.reduce_sum(r[1], r[2], axis=AX.X)
        elif k == "recip":
            ins = eng.reciprocal(r[1], r[2])
        else:
            raise ValueError(k)
        if r[-1] is not None:
            ins.then_inc(r[-1][0], r[-1][1])


def build_nc():
    nc = bass.Bass("TRN2", target_bir_lowering=False)

    x_in = nc.dram_tensor("x", [NSEQ, S, D], F32, kind="ExternalInput")
    wp_in = nc.dram_tensor("wp", [3 * D, D], F32, kind="ExternalInput")
    wo_in = nc.dram_tensor("wo", [D, D], F32, kind="ExternalInput")
    wg_in = nc.dram_tensor("wg", [IFF, D], I32, kind="ExternalInput")
    wu_in = nc.dram_tensor("wu", [IFF, D], I32, kind="ExternalInput")
    wd_in = nc.dram_tensor("wd", [D, IFF], I32, kind="ExternalInput")
    gs_in = nc.dram_tensor("gs", [128, ICN], F32, kind="ExternalInput")
    us_in = nc.dram_tensor("us", [128, ICN], F32, kind="ExternalInput")
    dsb_in = nc.dram_tensor("dsb", [128, D], F32, kind="ExternalInput")
    mask_in = nc.dram_tensor("mask", [128, 128], F32, kind="ExternalInput")
    smsc_in = nc.dram_tensor("smsc", [128, 1], F32, kind="ExternalInput")
    onesc_in = nc.dram_tensor("onesc", [128, 1], BF16, kind="ExternalInput")
    onesr_in = nc.dram_tensor("onesr", [1, 128], F32, kind="ExternalInput")
    y_out = nc.dram_tensor("y", [NSEQ, S, D], F32, kind="ExternalOutput")

    wp_bf = nc.dram_tensor("wp_bf", [3 * D, D], BF16, kind="Internal")
    wo_bf = nc.dram_tensor("wo_bf", [D, D], BF16, kind="Internal")
    wg_bf = nc.dram_tensor("wg_bf", [IFF, D], BF16, kind="Internal")
    wu_bf = nc.dram_tensor("wu_bf", [IFF, D], BF16, kind="Internal")
    wd_bf = nc.dram_tensor("wd_bf", [D, IFF], BF16, kind="Internal")
    cwp = nc.dram_tensor("cwp", [DC, 128, 3 * D], BF16, kind="Internal")
    cwo = nc.dram_tensor("cwo", [DC, 128, D], BF16, kind="Internal")
    cwg = nc.dram_tensor("cwg", [DC, 128, IFF], BF16, kind="Internal")
    cwu = nc.dram_tensor("cwu", [DC, 128, IFF], BF16, kind="Internal")
    cwd = nc.dram_tensor("cwd", [ICN, 128, D], BF16, kind="Internal")
    xnd = nc.dram_tensor("xnd", [S, D], BF16, kind="Internal")
    vd = nc.dram_tensor("vd", [DC, 128, HD], BF16, kind="Internal")
    aod = nc.dram_tensor("aod", [NH, HD, S], BF16, kind="Internal")
    gd = nc.dram_tensor("gd", [S, D], BF16, kind="Internal")

    taps = {}
    for t in TAPS:
        if t == "lnstats":
            taps[t] = nc.dram_tensor("tap_lnstats", [NT128, 4, 128], F32,
                                     kind="ExternalOutput")
            continue
        shp = {"xn": [S, D], "g": [S, D], "ao": [NH, HD, S]}[t.rstrip("01")]
        taps[t] = nc.dram_tensor("tap_" + t, shp, BF16, kind="ExternalOutput")

    A0 = 18432
    my_sbuf = set()

    def sb(name, shape, dtype, off):
        my_sbuf.add(name)
        return nc.alloc_sbuf_tensor_at(name, shape, dtype, offset=off)

    xnT = sb("xnT", [128, DC * S], BF16, A0)
    h_all = sb("h_all", [128, ICN * 512], BF16, A0 + 65536)
    st = A0 + 131072
    gs_sb = sb("gs_sb", [128, ICN], F32, st); st += 256
    us_sb = sb("us_sb", [128, ICN], F32, st); st += 256
    mask_sb = sb("mask_sb", [128, 128], F32, st); st += 512
    dsb_sb = sb("dsb_sb", [128, D], BF16, st); st += 4096
    onesc_sb = sb("onesc_sb", [128, 1], BF16, st); st += 32
    onesr_sb = sb("onesr_sb", [1, 128], F32, st); st += 512
    sm_t = sb("sm_t", [128, 1], F32, st); st += 32
    nmean_t = sb("nmean_t", [128, 1], F32, st); st += 32
    ssq_t = sb("ssq_t", [128, 1], F32, st); st += 32
    var_t = sb("var_t", [128, 1], F32, st); st += 32
    lnv_t = sb("lnv_t", [128, 1], F32, st); st += 32
    rstd_t = sb("rstd_t", [128, 1], F32, st); st += 32
    smsc_sb = sb("smsc_sb", [128, 1], F32, st); st += 32
    st = (st + 127) & ~127

    S1 = st
    xa_tile = sb("xa_tile", [128, D], F32, S1)
    xn_tile = sb("xn_tile", [128, D], BF16, S1 + 8192)
    q_sb = sb("q_sb", [128, S], BF16, S1)
    k_sb = sb("k_sb", [128, S], BF16, S1 + 4096)
    vtm_sb = sb("vtm_sb", [128, DC * HD], BF16, S1 + 8192)
    v_sb = sb("v_sb", [128, S], BF16, S1 + 12288)
    attT_sb = sb("attT_sb", [128, 4096], BF16, S1 + 16384)
    wq_st = sb("wq_st", [128, DC * 128], BF16, S1 + 24576)
    wk_st = sb("wk_st", [128, DC * 128], BF16, S1 + 28672)
    wv_st = sb("wv_st", [128, DC * 128], BF16, S1 + 32768)
    ao_st0 = sb("ao_st0", [128, 512], BF16, S1 + 36864)
    ao_st1 = sb("ao_st1", [128, 512], BF16, S1 + 37888)
    S1_END = S1 + 38912

    S2 = S1_END
    zr_sb = sb("zr_sb", [1, 512], F32, S2)
    zbc_sb = sb("zbc_sb", [128, 512], F32, S2 + 2048)
    bt0 = sb("bt0", [128, 4096], BF16, S2 + 4096)
    bt1 = sb("bt1", [128, 4096], BF16, S2 + 12288)
    wg_st = [sb(f"wg_st{i}", [128, DC * 128], BF16, S2 + 4096 * i) for i in range(2)]
    wu_st = [sb(f"wu_st{i}", [128, DC * 128], BF16, S2 + 8192 + 4096 * i) for i in range(2)]
    wd_st = [sb(f"wd_st{i}", [128, 2048], BF16, S2 + 16384 + 4096 * i) for i in range(2)]
    silu_tmp = sb("silu_tmp", [128, 512], BF16, S2 + 24576)
    dtmp = sb("dtmp", [128, 512], F32, S2 + 25600)
    g_st = [sb(f"g_st{i}", [128, 1024], BF16, S2 + 27648 + 2048 * i) for i in range(2)]
    woq_st = sb("woq_st", [128, DC * 512], BF16, S2)
    ao_ld = [sb(f"ao_ld{i}", [128, DC * 128], BF16, S2 + 16384 + 4096 * i) for i in range(2)]
    xc_ld = [sb(f"xc_ld{i}", [128, 512], F32, S2 + 24576 + 2048 * i) for i in range(2)]
    gc_ld = [sb(f"gc_ld{i}", [128, 512], BF16, S2 + 28672 + 1024 * i) for i in range(2)]
    cmb = [sb(f"cmb{i}", [128, 512], F32, S2 + 30720 + 2048 * i) for i in range(2)]
    S2_END = S2 + 34816
    assert S2_END <= 229344, f"SBUF overflow: {S2_END}"

    es = ExitStack()
    with es:
        psumA = es.enter_context(nc.psum_tensor("psumA", [128, 2048], F32))
        psumB = es.enter_context(nc.psum_tensor("psumB", [128, 2048], F32))
        b = Builder(nc, es)

        ct = {}
        ct["gs"] = b.dma("sync", gs_sb[:], gs_in.ap())
        ct["us"] = b.dma("sync", us_sb[:], us_in.ap())
        ct["mask"] = b.dma("sync", mask_sb[:], mask_in.ap())
        ct["smsc"] = b.dma("sync", smsc_sb[:], smsc_in.ap())
        ct["onesc"] = b.dma("sync", onesc_sb[:], onesc_in.ap())
        ct["onesr"] = b.dma("sync", onesr_sb[:], onesr_in.ap())
        ct["dsb"] = b.dma("gpsimd", dsb_sb[:], dsb_in.ap())

        cast_toks = {}
        for nm, src, dst, rows in (("wp", wp_in, wp_bf, 3 * D), ("wo", wo_in, wo_bf, D),
                                   ("wg", wg_in, wg_bf, IFF), ("wu", wu_in, wu_bf, IFF),
                                   ("wd", wd_in, wd_bf, D)):
            toks = []
            for ch in range(rows // 2048):
                toks.append(b.dma("gpsimd", dst[ch * 2048:(ch + 1) * 2048, :],
                                  src[ch * 2048:(ch + 1) * 2048, :]))
            cast_toks[nm] = toks

        bt_free = [0, 0]
        cache_toks = {"wp": [], "wo": [], "wg": [], "wu": [], "wd": []}

        def build_block(nm, src_bf, cache, j, hf, rows, minus128):
            bid = hf
            buf = (bt0, bt1)[bid]
            half = rows // 2
            if j == 0:
                b.waitt("sync", cast_toks[nm])
            b.waitt("sync", bt_free[bid])
            vt = b.dma("sync", buf[:, 0:half],
                       src_bf[hf * half:(hf + 1) * half, j * 128:(j + 1) * 128],
                       transpose=True)
            if minus128:
                b.waitt("vector", vt)
                vv = b.ts(buf[:, 0:half], buf[:, 0:half], -128.0, None,
                          ALU.add, ALU.bypass)
                b.wait("gpsimd", b.c["vector"], vv)
            else:
                b.waitt("gpsimd", vt)
            tok = b.dma("gpsimd", cache[j, :, hf * half:(hf + 1) * half], buf[:, 0:half])
            bt_free[bid] = tok
            cache_toks[nm].append(tok)

        def build_all(nm, src_bf, cache, nblk, rows, minus128):
            for j in range(nblk):
                for hf in range(2):
                    build_block(nm, src_bf, cache, j, hf, rows, minus128)

        ffn_build_jobs = []
        if set(RUN_PHASES) & {"B", "D", "C"}:
            for nm, src_bf, cache, nblk, rows, m128 in (
                    ("wg", wg_bf, cwg, DC, IFF, True),
                    ("wu", wu_bf, cwu, DC, IFF, True),
                    ("wd", wd_bf, cwd, ICN, D, True)):
                for j in range(nblk):
                    for hf in range(2):
                        ffn_build_jobs.append(
                            (lambda nm=nm, s_=src_bf, c_=cache, j=j, hf=hf,
                                    r=rows, m=m128: build_block(nm, s_, c_, j, hf, r, m)))

        def phase_A(s, G, t_xnT_prev):
            x_s = x_in[s]
            b.wait("sync", b.c["tensor"], G.get("B_t", 0))
            b.wait("sync", b.c["vector"], G.get("B_v", 0))
            b.wait("sync", b.c["scalar"], G.get("B_s", 0))
            store_toks = []
            v_sq = v_ap = 0
            for t in range(NT128):
                b.wait("sync", b.c["scalar"], v_sq)
                b.wait("sync", b.c["vector"], v_ap)
                if store_toks:
                    b.waitt("sync", store_toks[-1])
                vx = b.dma("sync", xa_tile[:], x_s[t * 128:(t + 1) * 128, :])
                b.waitt("vector", vx)
                v1 = b.red(sm_t[:], xa_tile[:])
                b.wait("vector", b.c["vector"], v1)
                v2 = b.ts(nmean_t[:], sm_t[:], -1.0 / D, None, ALU.mult, ALU.bypass)
                b.wait("scalar", b.c["vector"], v2)
                if store_toks:
                    b.waitt("scalar", store_toks[-1])
                v_sq = b.act(xn_tile[:], xa_tile[:], AF.Square, bias=nmean_t[:])
                b.wait("vector", b.c["scalar"], v_sq)
                v_red = b.red(ssq_t[:], xn_tile[:])
                b.wait("vector", b.c["vector"], v_red)
                v3 = b.ts(var_t[:], ssq_t[:], 1.0 / D, LN_EPS, ALU.mult, ALU.add)
                b.wait("scalar", b.c["vector"], v3)
                v4 = b.act(lnv_t[:], var_t[:], AF.Ln)
                b.wait("scalar", b.c["scalar"], v4)
                v5 = b.act(rstd_t[:], lnv_t[:], AF.Exp, scale=-0.5)
                b.wait("vector", b.c["scalar"], v5)
                v_ap = b.ts(xn_tile[:], xa_tile[:], nmean_t[:], rstd_t[:],
                            ALU.add, ALU.mult)
                b.wait("gpsimd", b.c["vector"], v_ap)
                if t == 0:
                    b.waitt("gpsimd", t_xnT_prev)
                store_toks.append(b.dma("gpsimd", xnd[t * 128:(t + 1) * 128, :],
                                        xn_tile[:]))
                if "lnstats" in taps and s == RUN_SEQS[-1]:
                    for i, tile in enumerate((nmean_t, lnv_t, var_t, rstd_t)):
                        store_toks.append(
                            b.dma("gpsimd", taps["lnstats"][t, i, :], tile[:, 0]))
            b.waitt("sync", store_toks)
            b.wait("sync", b.c["tensor"], G.get("D_t", 0))
            t_xnT = []
            for dc in range(DC):
                t_xnT.append(b.dma("sync", xnT[:, dc * S:(dc + 1) * S],
                                   xnd[:, dc * 128:(dc + 1) * 128], transpose=True))
            if f"xn{s}" in taps:
                b.waitt("gpsimd", store_toks)
                b.dma("gpsimd", taps[f"xn{s}"].ap(), xnd.ap())
            return t_xnT

        BST = {"t_vtm": 0, "t_ao": [0, 0]}

        def phase_B(s, G, t_xnT, t_aold_prev, background):
            v_wst = G.get("B_t", 0)
            t_vtm = BST["t_vtm"]
            t_ao = BST["t_ao"]
            v_aoev_slot = [G.get("C_v", 0)] * 2
            v_exp_slot = [G.get("B_s", 0)] * 3
            v_attT_slot = [G.get("C_t", 0)] * 8
            v_evac_slot = [G.get("C_v", 0)] * 2
            v_vcp = G.get("C_v", 0)
            v_exp_last = 0
            t_aod_all = []
            first_xnT = [True]
            pending = [None]
            QP = (psumA[:, 0:512], psumA[:, 512:1024])
            AV = (psumA[:, 1024:1536], psumA[:, 1536:2048])
            SC = (psumB[:, 0:512], psumB[:, 512:1024], psumB[:, 1024:1536])
            ZS = psumB[0:1, 1536:2048]
            ZB = psumB[:, 1536:2048]
            if s == 0:
                b.waitt("scalar", ct["smsc"])
                b.waitt("tensor", [ct["onesc"], ct["onesr"]])
                b.waitt("vector", ct["mask"])
                b.waitt("sync", cache_toks["wp"])
            b.wait("tensor", b.c["vector"], G.get("C_v", 0))

            def emit_finish(fin):
                qg, h, v_end = fin
                av = qg % 2
                b.wait("vector", b.c["tensor"], v_end)
                vrec = b.recip(zr_sb[:], ZS)
                b.wait("tensor", b.c["vector"], vrec)
                vbc = b.mm(ZB, onesr_sb[:], zr_sb[:], start=True, stop=True, inc=1)
                b.wait("vector", b.c["tensor"], vbc)
                vcp = b.copy(zbc_sb[:], ZB)
                ao_buf = (ao_st0, ao_st1)[av]
                b.wait("vector", b.c["vector"], vcp)
                b.waitt("vector", t_ao[av])
                v_aoev = b.tt(ao_buf[:], AV[av], zbc_sb[:], ALU.mult)
                b.wait("gpsimd", b.c["vector"], v_aoev)
                if h == 0 and qg == 0:
                    b.waitt("gpsimd", t_aold_prev)
                t_ao[av] = b.dma("gpsimd", aod[h, :, qg * 512:(qg + 1) * 512],
                                 ao_buf[:])
                t_aod_all.append(t_ao[av])
                return v_aoev, vcp

            for h in range(NH):
                b.wait("sync", b.c["tensor"], v_wst)
                t_wld = []
                for st_buf, part in ((wq_st, 0), (wk_st, 1), (wv_st, 2)):
                    off = part * D + h * HD
                    t_wld.append(b.dma(
                        "sync", st_buf[:].rearrange("p (c i) -> p c i", c=DC),
                        cwp[:, :, off:off + HD].rearrange("c p i -> p c i")))

                def proj(st_buf, evac_to):
                    b.waitt("tensor", t_wld)
                    if first_xnT[0]:
                        b.waitt("tensor", t_xnT)
                        first_xnT[0] = False
                    last_evac = 0
                    vmm = 0
                    for sl in range(4):
                        qp = QP[sl % 2]
                        b.wait("tensor", b.c["vector"], v_evac_slot[sl % 2])
                        for dc in range(DC):
                            vmm = b.mm(qp, st_buf[:, dc * 128:(dc + 1) * 128],
                                       xnT[:, dc * S + sl * 512:dc * S + (sl + 1) * 512],
                                       start=(dc == 0), stop=(dc == DC - 1),
                                       inc=(1 if dc == DC - 1 else 0))
                        b.wait("vector", b.c["tensor"], vmm)
                        v_evac_slot[sl % 2] = b.copy(
                            evac_to[:, sl * 512:(sl + 1) * 512], qp)
                        last_evac = v_evac_slot[sl % 2]
                    return last_evac, vmm

                v_q_evac, _ = proj(wq_st, q_sb)
                v_k_evac, _ = proj(wk_st, k_sb)
                v_v_evac, vv_mm = proj(wv_st, v_sb)
                v_wst = vv_mm
                b.wait("gpsimd", b.c["vector"], v_v_evac)
                b.waitt("gpsimd", t_vtm)
                t_vst = b.dma("gpsimd", vd.rearrange("c h t -> h c t"),
                              v_sb[:].rearrange("p (c t) -> p c t", c=DC))
                b.waitt("sync", t_vst)
                b.wait("sync", b.c["tensor"], v_attT_slot[0])
                t_vtm = b.dma("sync", vtm_sb[:], vd.rearrange("c h t -> (c h) t"),
                              transpose=True)

                for qg in range(4):
                    if background:
                        for _ in range(3):
                            if ffn_build_jobs:
                                ffn_build_jobs.pop(0)()
                    nch = 4 * (qg + 1)
                    v_exp_chunk = [0] * nch
                    first_z = [True]

                    def emit_scores(c):
                        qlo = max(0, c * 128 - qg * 512)
                        sc = SC[c % 3]
                        b.wait("tensor", b.c["scalar"], v_exp_slot[c % 3])
                        if qg == 0 and c == 0:
                            b.wait("tensor", b.c["vector"], v_v_evac)
                        vsc = b.mm(sc[:, qlo:512], k_sb[:, c * 128:(c + 1) * 128],
                                   q_sb[:, qg * 512 + qlo:(qg + 1) * 512],
                                   start=True, stop=True, inc=1)
                        diag = c - 4 * qg
                        if 0 <= diag <= 3:
                            b.wait("vector", b.c["tensor"], vsc)
                            vmk = b.tt(sc[:, qlo:qlo + 128], sc[:, qlo:qlo + 128],
                                       mask_sb[:], ALU.add)
                            b.wait("scalar", b.c["vector"], vmk)
                        else:
                            b.wait("scalar", b.c["tensor"], vsc)
                        b.wait("scalar", b.c["tensor"], v_attT_slot[c % 8])
                        v_exp_chunk[c] = b.act(
                            attT_sb[:, (c % 8) * 512 + qlo:(c % 8) * 512 + 512],
                            sc[:, qlo:512], AF.Exp, scale=smsc_sb[:])
                        v_exp_slot[c % 3] = v_exp_chunk[c]

                    def emit_zattv(c):
                        qlo = max(0, c * 128 - qg * 512)
                        b.wait("tensor", b.c["scalar"], v_exp_chunk[c])
                        if first_z[0]:
                            b.wait("tensor", b.c["vector"], v_vcp)
                            b.wait("tensor", b.c["vector"], v_aoev_slot[qg % 2])
                            if qg == 0:
                                b.waitt("tensor", t_vtm)
                            first_z[0] = False
                        rhs = attT_sb[:, (c % 8) * 512 + qlo:(c % 8) * 512 + 512]
                        b.mm(ZS[:, qlo:512], onesc_sb[:], rhs,
                             start=(c == 0), stop=(c == nch - 1))
                        v_attT_slot[c % 8] = b.mm(
                            AV[qg % 2][:, qlo:512], vtm_sb[:, c * 128:(c + 1) * 128],
                            rhs, start=(c == 0), stop=(c == nch - 1), inc=1)

                    for c in range(nch + 2):
                        if c < nch:
                            emit_scores(c)
                        if c == 2 and pending[0] is not None:
                            v_ae, v_vcp = emit_finish(pending[0])
                            v_aoev_slot[pending[0][0] % 2] = v_ae
                            pending[0] = None
                        if c >= 2:
                            emit_zattv(c - 2)
                    v_exp_last = max(v_exp_slot)
                    pending[0] = (qg, h, b.c["tensor"].n)
            if pending[0] is not None:
                v_ae, v_vcp = emit_finish(pending[0])
                pending[0] = None
            BST["t_vtm"] = t_vtm
            BST["t_ao"] = t_ao
            if f"ao{s}" in taps:
                b.waitt("gpsimd", t_aod_all)
                b.dma("gpsimd", taps[f"ao{s}"].ap(), aod.ap())
            return t_aod_all

        def phase_D(s, G, t_gc_prev):
            t_gstore = [0, 0]
            v_w_ld = [G.get("B_t", 0)] * 2
            v_wd_ld = [G.get("B_t", 0)] * 2
            v_sil_bid = [G.get("B_s", 0)] * 2
            v_h_bid = [G.get("B_v", 0)] * 2
            v_devac = G.get("B_v", 0)
            v_dsilu = G.get("C_s", 0)
            t_gd = {}
            b.wait("sync", b.c["vector"], G.get("B_v", 0))
            if s == 0:
                b.waitt("scalar", ct["gs"])
                b.waitt("vector", [ct["us"], ct["dsb"]])
                b.waitt("sync", cache_toks["wg"] + cache_toks["wu"] + cache_toks["wd"])
            for tc in range(TCN):
                t0 = tc * 512
                for ic in range(ICN):
                    bid = ic % 2
                    b.wait("sync", b.c["tensor"], v_w_ld[bid])
                    tg = b.dma("sync",
                               wg_st[bid][:].rearrange("p (c i) -> p c i", c=DC),
                               cwg[:, :, ic * 128:(ic + 1) * 128].rearrange("c p i -> p c i"))
                    tu = b.dma("sync",
                               wu_st[bid][:].rearrange("p (c i) -> p c i", c=DC),
                               cwu[:, :, ic * 128:(ic + 1) * 128].rearrange("c p i -> p c i"))
                    gsl = bid * 512
                    usl = 1024 + bid * 512
                    b.waitt("tensor", [tg, tu])
                    b.wait("tensor", b.c["scalar"], v_sil_bid[bid])
                    b.wait("tensor", b.c["vector"], v_h_bid[bid])
                    if ic < 2:
                        b.wait("tensor", b.c["vector"], v_devac)
                    v_gmm = v_umm = 0
                    for dc in range(DC):
                        v_gmm = b.mm(psumA[:, gsl:gsl + 512],
                                     wg_st[bid][:, dc * 128:(dc + 1) * 128],
                                     xnT[:, dc * S + t0:dc * S + t0 + 512],
                                     start=(dc == 0), stop=(dc == DC - 1),
                                     inc=(1 if dc == DC - 1 else 0))
                    for dc in range(DC):
                        v_umm = b.mm(psumA[:, usl:usl + 512],
                                     wu_st[bid][:, dc * 128:(dc + 1) * 128],
                                     xnT[:, dc * S + t0:dc * S + t0 + 512],
                                     start=(dc == 0), stop=(dc == DC - 1),
                                     inc=(1 if dc == DC - 1 else 0))
                    v_w_ld[bid] = v_umm
                    b.wait("scalar", b.c["tensor"], v_gmm)
                    b.wait("scalar", b.c["vector"], v_h_bid[1 - bid])
                    v_sil_bid[bid] = b.act(silu_tmp[:], psumA[:, gsl:gsl + 512],
                                           AF.Silu, scale=gs_sb[:, ic:ic + 1])
                    b.wait("vector", b.c["scalar"], v_sil_bid[bid])
                    b.wait("vector", b.c["tensor"], v_umm)
                    v_h_bid[bid] = b.stt(h_all[:, ic * 512:(ic + 1) * 512],
                                         psumA[:, usl:usl + 512],
                                         us_sb[:, ic:ic + 1], silu_tmp[:],
                                         ALU.mult, ALU.mult)
                v_h_all = v_h_bid[(ICN - 1) % 2]
                for hh in range(2):
                    hof = hh * 1024
                    for ic in range(ICN):
                        pair, sub = ic // 2, ic % 2
                        bid = pair % 2
                        if sub == 0:
                            b.wait("sync", b.c["tensor"], v_wd_ld[bid])
                            tw = b.dma("sync",
                                       wd_st[bid][:].rearrange("p (c h) -> p c h", c=2),
                                       cwd[ic:ic + 2, :, hof:hof + 1024].rearrange("c p h -> p c h"))
                            tw_cur = tw
                            b.waitt("tensor", tw_cur)
                        if ic == 0:
                            b.wait("tensor", b.c["vector"], v_h_all)
                            b.wait("tensor", b.c["vector"], v_devac)
                        for tl in range(4):
                            ps = (psumA, psumA, psumB, psumB)[tl]
                            po = (0, 1024, 0, 1024)[tl]
                            for n in range(2):
                                b.mm(ps[:, po + n * 512:po + (n + 1) * 512],
                                     h_all[:, ic * 512 + tl * 128:ic * 512 + (tl + 1) * 128],
                                     wd_st[bid][:, sub * 1024 + n * 512:sub * 1024 + (n + 1) * 512],
                                     start=(ic == 0), stop=(ic == ICN - 1),
                                     inc=(1 if (tl == 3 and n == 1) else 0))
                        if sub == 1:
                            v_wd_ld[bid] = b.c["tensor"].n
                    v_dmm_all = b.c["tensor"].n
                    for tl in range(4):
                        ps = (psumA, psumA, psumB, psumB)[tl]
                        po = (0, 1024, 0, 1024)[tl]
                        gb = tl % 2
                        for hf2 in range(2):
                            b.wait("vector", b.c["tensor"], v_dmm_all)
                            b.wait("vector", b.c["scalar"], v_dsilu)
                            v_devac = b.tt(dtmp[:], ps[:, po + hf2 * 512:po + (hf2 + 1) * 512],
                                           dsb_sb[:, hof + hf2 * 512:hof + (hf2 + 1) * 512],
                                           ALU.mult)
                            b.wait("scalar", b.c["vector"], v_devac)
                            if hf2 == 0:
                                b.waitt("scalar", t_gstore[gb])
                            v_dsilu = b.act(g_st[gb][:, hf2 * 512:(hf2 + 1) * 512],
                                            dtmp[:], AF.Silu)
                        b.wait("gpsimd", b.c["scalar"], v_dsilu)
                        tglob = tc * 4 + tl
                        if tglob < 2 and hh == 0:
                            b.waitt("gpsimd", t_gc_prev)
                            t_gc_prev = 0
                        tok = b.dma(
                            "gpsimd",
                            gd[t0 + tl * 128:t0 + (tl + 1) * 128, hof:hof + 1024],
                            g_st[gb][:])
                        t_gstore[gb] = tok
                        t_gd.setdefault(tglob, []).append(tok)
            if f"g{s}" in taps:
                b.waitt("gpsimd", [t for t in t_gstore if t])
                b.dma("gpsimd", taps[f"g{s}"].ap(), gd.ap())
            return t_gd

        def phase_C(s, G, t_gd, t_aod_all):
            t_x = [0, 0]
            t_g = [0, 0]
            t_store = [0, 0]
            v_cmb = [G.get("D_v", 0)] * 2
            v_mm = [G.get("D_t", 0)] * 2
            t_gc_all = []
            t_aold_all = []
            b.wait("sync", b.c["scalar"], G.get("D_s", 0))
            b.wait("sync", b.c["vector"], G.get("D_v", 0))
            b.waitt("sync", t_aod_all)
            b.waitt("sync", [tk for toks in t_gd.values() for tk in toks])
            if s == 0:
                b.waitt("sync", cache_toks["wo"])
            first_cmb = [True]
            for oq in range(4):
                oof = oq * 512
                b.wait("sync", b.c["tensor"], max(v_mm))
                t_wo = b.dma("sync", woq_st[:].rearrange("p (c o) -> p c o", c=DC),
                             cwo[:, :, oof:oof + 512].rearrange("c p o -> p c o"))
                for t in range(NT128):
                    bid = t % 2
                    b.wait("sync", b.c["tensor"], v_mm[bid])
                    b.wait("sync", b.c["vector"], v_cmb[bid])
                    t_ao_ld = b.dma(
                        "sync", ao_ld[bid][:].rearrange("p (c i) -> p c i", c=DC),
                        aod[:, :, t * 128:(t + 1) * 128].rearrange("c p i -> p c i"))
                    t_aold_all.append(t_ao_ld)
                    t_x[bid] = b.dma("sync", xc_ld[bid][:],
                                     x_in[s][t * 128:(t + 1) * 128, oof:oof + 512])
                    t_g[bid] = b.dma("sync", gc_ld[bid][:],
                                     gd[t * 128:(t + 1) * 128, oof:oof + 512])
                    t_gc_all.append(t_g[bid])
                    ps = (psumA, psumB)[bid]
                    b.waitt("tensor", [t_wo, t_ao_ld])
                    b.wait("tensor", b.c["vector"], v_cmb[bid])
                    vm = 0
                    for dc in range(DC):
                        vm = b.mm(ps[:, 0:512], ao_ld[bid][:, dc * 128:(dc + 1) * 128],
                                  woq_st[:, dc * 512:(dc + 1) * 512],
                                  start=(dc == 0), stop=(dc == DC - 1),
                                  inc=(1 if dc == DC - 1 else 0))
                    v_mm[bid] = vm
                    b.wait("vector", b.c["tensor"], vm)
                    if first_cmb[0]:
                        b.wait("vector", b.c["scalar"], G.get("D_s", 0))
                        first_cmb[0] = False
                    b.waitt("vector", [t_x[bid], t_g[bid], t_store[bid]])
                    vc1 = b.stt(cmb[bid][:], ps[:, 0:512], 1.0, gc_ld[bid][:],
                                ALU.mult, ALU.mult)
                    b.wait("vector", b.c["vector"], vc1)
                    v_cmb[bid] = b.tt(cmb[bid][:], cmb[bid][:], xc_ld[bid][:], ALU.add)
                    b.wait("gpsimd", b.c["vector"], v_cmb[bid])
                    t_store[bid] = b.dma(
                        "gpsimd", y_out[s][t * 128:(t + 1) * 128, oof:oof + 512],
                        cmb[bid][:])
            return t_gc_all, t_aold_all

        G = {}
        t_xnT_prev = 0
        t_aold_prev = 0
        t_gc_prev = 0
        first = True
        for s in RUN_SEQS:
            t_xnT = phase_A(s, G, t_xnT_prev) if "A" in RUN_PHASES else []
            t_xnT_prev = t_xnT
            if first and set(RUN_PHASES) & {"B", "C"}:
                build_all("wp", wp_bf, cwp, DC, 3 * D, False)
                build_all("wo", wo_bf, cwo, DC, D, False)
            if "B" in RUN_PHASES:
                t_aod_all = phase_B(s, G, t_xnT, t_aold_prev, background=first)
                if first:
                    while ffn_build_jobs:
                        ffn_build_jobs.pop(0)()
                G["B_t"], G["B_v"], G["B_s"] = (b.c["tensor"].n, b.c["vector"].n,
                                                b.c["scalar"].n)
            else:
                t_aod_all = []
            if "D" in RUN_PHASES:
                t_gd = phase_D(s, G, t_gc_prev)
                G["D_t"], G["D_v"], G["D_s"] = (b.c["tensor"].n, b.c["vector"].n,
                                                b.c["scalar"].n)
            else:
                t_gd = {}
            if "C" in RUN_PHASES:
                t_gc_prev, t_aold_prev = phase_C(s, G, t_gd, t_aod_all)
                G["C_t"], G["C_v"], G["C_s"] = (b.c["tensor"].n, b.c["vector"].n,
                                                b.c["scalar"].n)
            first = False
        b.barrier()

        with nc.Block() as block:
            @block.sync
            def _(eng):
                _replay(eng, b.streams["sync"])

            @block.gpsimd
            def _(eng):
                _replay(eng, b.streams["gpsimd"])

            @block.vector
            def _(eng):
                _replay(eng, b.streams["vector"])

            @block.scalar
            def _(eng):
                _replay(eng, b.streams["scalar"])

            @block.tensor
            def _(eng):
                _replay(eng, b.streams["tensor"])

    for f in nc.m.functions:
        for alloc in f.allocations:
            locs = getattr(alloc, "memorylocations", None)
            if not locs:
                continue
            for loc in locs:
                if loc.type != "SB":
                    continue
                base = loc.name.split("_set")[0]
                parts = base.rsplit("_", 1)
                if len(parts) == 2 and parts[1].isdigit():
                    base = parts[0]
                if base in my_sbuf:
                    continue
                sz = 1
                for d_ in (loc.dims or [1]):
                    sz *= d_
                if loc.name.startswith("DynamicDMAScratch"):
                    continue
                assert loc.addr + sz <= A0 or loc.addr >= S2_END, \
                    f"auto SBUF alloc {loc.name} at {loc.addr} overlaps manual region"
    return nc


def kernel(L0_attn_x, L0_attn_w_project, L0_attn_w_out, L0_ffn_gate_w,
           L0_ffn_gate_scale, L0_ffn_up_w, L0_ffn_up_scale, L0_ffn_down_w,
           L0_ffn_down_scale):
    x = np.ascontiguousarray(np.asarray(L0_attn_x, dtype=np.float32))
    wp = np.ascontiguousarray(np.asarray(L0_attn_w_project, dtype=np.float32))
    wo = np.ascontiguousarray(np.asarray(L0_attn_w_out, dtype=np.float32))
    wg = np.ascontiguousarray(np.asarray(L0_ffn_gate_w, dtype=np.int32))
    wu = np.ascontiguousarray(np.asarray(L0_ffn_up_w, dtype=np.int32))
    wd = np.ascontiguousarray(np.asarray(L0_ffn_down_w, dtype=np.int32))
    gs = np.ascontiguousarray(np.asarray(L0_ffn_gate_scale, dtype=np.float32).reshape(ICN, 128).T)
    us = np.ascontiguousarray(np.asarray(L0_ffn_up_scale, dtype=np.float32).reshape(ICN, 128).T)
    ds = np.asarray(L0_ffn_down_scale, dtype=np.float32).reshape(1, -1)
    dsb = np.ascontiguousarray(np.broadcast_to(ds, (128, D)))
    mask = np.ascontiguousarray(np.tril(np.full((128, 128), -1e9, dtype=np.float32), k=-1))
    smsc = np.full((128, 1), SM_SCALE, dtype=np.float32)
    onesc = np.ones((128, 1), dtype=bfloat16)
    onesr = np.ones((1, 128), dtype=np.float32)

    nc = build_nc()
    in_maps = []
    for c in range(NCORE):
        in_maps.append({"x": x[NSEQ * c:NSEQ * (c + 1)], "wp": wp, "wo": wo,
                        "wg": wg, "wu": wu, "wd": wd, "gs": gs, "us": us,
                        "dsb": dsb, "mask": mask, "smsc": smsc,
                        "onesc": onesc, "onesr": onesr})
    res = run_bass_kernel_spmd(nc, in_maps, core_ids=list(range(NCORE)))
    out = np.concatenate([r["y"] for r in res.results], axis=0)
    if TAPS:
        kernel.tap_results = [{k: v for k, v in r.items() if k.startswith("tap_")}
                              for r in res.results]
    return out


# version 06
# speedup vs baseline: 1.5207x; 1.0528x over previous
import numpy as np
import ml_dtypes
from contextlib import ExitStack

import concourse.bass as bass
import concourse.mybir as mybir
from concourse.bass_utils import run_bass_kernel_spmd

F32 = mybir.dt.float32
BF16 = mybir.dt.bfloat16
I32 = mybir.dt.int32
AF = mybir.ActivationFunctionType
ALU = mybir.AluOpType
AX = mybir.AxisListType
bfloat16 = ml_dtypes.bfloat16

D = 2048
S = 2048
NH = 16
HD = 128
IFF = 8192
NCORE = 8
NSEQ = 2
DC = D // 128
ICN = IFF // 128
TCN = 4
NT128 = 16
SM_SCALE = 1.0 / float(np.sqrt(HD))
LN_EPS = 1e-5
POOLN = 24

TAPS = ()
RUN_SEQS = list(range(NSEQ))
RUN_PHASES = ("A", "B", "D", "C")

ENGINES = ("sync", "gpsimd", "vector", "scalar", "tensor")


class Sem:
    __slots__ = ("h", "n")

    def __init__(self, handle):
        self.h = handle
        self.n = 0


class Builder:
    def __init__(self, nc, es):
        self.nc = nc
        self.streams = {e: [] for e in ENGINES}
        self.c = {e: Sem(es.enter_context(nc.semaphore(f"c_{e}"))) for e in ENGINES}
        self.pool = {q: [Sem(es.enter_context(nc.semaphore(f"p_{q}{i}")))
                         for i in range(POOLN)]
                     for q in ("sync", "gpsimd", "scalar")}
        self.pidx = {"sync": 0, "gpsimd": 0, "scalar": 0}

    def wait(self, eng, ctr, val):
        if val > 0:
            self.streams[eng].append(["wait", ctr.h, val])

    def waitt(self, eng, tok):
        if not tok:
            return
        if isinstance(tok, list):
            for t in tok:
                self.waitt(eng, t)
            return
        self.streams[eng].append(["wait", tok[0].h, tok[1]])

    def _emit(self, eng, rec, inc):
        ctr = self.c[eng]
        if inc:
            ctr.n += inc
            rec.append((ctr.h, inc))
        else:
            rec.append(None)
        self.streams[eng].append(rec)
        return ctr.n

    def dma(self, eng, out, in_, transpose=False):
        pool = self.pool[eng]
        s = pool[self.pidx[eng] % POOLN]
        self.pidx[eng] += 1
        if s.n > 0:
            self.streams[eng].append(["wait", s.h, s.n])
        s.n += 16
        self.streams[eng].append(["dma", out, in_, transpose, (s.h, 16)])
        return (s, s.n)

    def mm(self, out, lhsT, rhs, start, stop, inc=0):
        return self._emit("tensor", ["mm", out, lhsT, rhs, start, stop], inc)

    def act(self, out, in_, func, bias=None, scale=None, inc=1):
        return self._emit("scalar", ["act", out, in_, func, bias, scale], inc)

    def copy(self, out, in_, inc=1, eng="vector"):
        return self._emit(eng, ["copy", out, in_], inc)

    def tt(self, out, a, b, op, inc=1, eng="vector"):
        return self._emit(eng, ["tt", out, a, b, op], inc)

    def ts(self, out, in_, s1, s2, op0, op1, inc=1, eng="vector"):
        return self._emit(eng, ["ts", out, in_, s1, s2, op0, op1], inc)

    def stt(self, out, in0, sc, in1, op0, op1, inc=1, eng="vector"):
        return self._emit(eng, ["stt", out, in0, sc, in1, op0, op1], inc)

    def red(self, out, in_, inc=1, eng="vector"):
        return self._emit(eng, ["red", out, in_], inc)

    def recip(self, out, in_, inc=1):
        return self._emit("vector", ["recip", out, in_], inc)

    def snap(self):
        return {e: self.c[e].n for e in ("tensor", "vector", "scalar")}

    def ensure_tail_inc(self, eng):
        for rec in reversed(self.streams[eng]):
            if rec[0] == "wait":
                continue
            if rec[0] == "dma":
                return
            if rec[-1] is None:
                ctr = self.c[eng]
                ctr.n += 1
                rec[-1] = (ctr.h, 1)
            return

    def barrier(self):
        for e in ENGINES:
            self.ensure_tail_inc(e)
        cvals = {e: self.c[e].n for e in ENGINES}
        pvals = [(s, s.n) for q in self.pool.values() for s in q if s.n > 0]
        for e in ENGINES:
            for o in ENGINES:
                if o != e and cvals[o] > 0:
                    self.wait(e, self.c[o], cvals[o])
            for s, v in pvals:
                self.streams[e].append(["wait", s.h, v])


def _replay(eng, ops):
    for r in ops:
        k = r[0]
        if k == "wait":
            eng.wait_ge(r[1], r[2])
            continue
        if k == "dma":
            ins = eng.dma_start(out=r[1], in_=r[2], transpose=r[3])
        elif k == "mm":
            ins = eng.matmul(r[1], r[2], r[3], start=r[4], stop=r[5])
        elif k == "act":
            kw = {}
            if r[4] is not None:
                kw["bias"] = r[4]
            if r[5] is not None:
                kw["scale"] = r[5]
            ins = eng.activation(r[1], r[2], r[3], **kw)
        elif k == "copy":
            ins = eng.tensor_copy(r[1], r[2])
        elif k == "tt":
            ins = eng.tensor_tensor(r[1], r[2], r[3], r[4])
        elif k == "ts":
            ins = eng.tensor_scalar(r[1], r[2], r[3], r[4], r[5], r[6])
        elif k == "stt":
            ins = eng.scalar_tensor_tensor(r[1], r[2], r[3], r[4], r[5], r[6])
        elif k == "red":
            ins = eng.reduce_sum(r[1], r[2], axis=AX.X)
        elif k == "recip":
            ins = eng.reciprocal(r[1], r[2])
        else:
            raise ValueError(k)
        if r[-1] is not None:
            ins.then_inc(r[-1][0], r[-1][1])


def build_nc():
    nc = bass.Bass("TRN2", target_bir_lowering=False)

    x_in = nc.dram_tensor("x", [NSEQ, S, D], F32, kind="ExternalInput")
    wp_in = nc.dram_tensor("wp", [3 * D, D], F32, kind="ExternalInput")
    wo_in = nc.dram_tensor("wo", [D, D], F32, kind="ExternalInput")
    wg_in = nc.dram_tensor("wg", [IFF, D], I32, kind="ExternalInput")
    wu_in = nc.dram_tensor("wu", [IFF, D], I32, kind="ExternalInput")
    wd_in = nc.dram_tensor("wd", [D, IFF], I32, kind="ExternalInput")
    gs_in = nc.dram_tensor("gs", [128, ICN], F32, kind="ExternalInput")
    us_in = nc.dram_tensor("us", [128, ICN], F32, kind="ExternalInput")
    dsb_in = nc.dram_tensor("dsb", [128, D], F32, kind="ExternalInput")
    mask_in = nc.dram_tensor("mask", [128, 128], F32, kind="ExternalInput")
    smsc_in = nc.dram_tensor("smsc", [128, 1], F32, kind="ExternalInput")
    onesc_in = nc.dram_tensor("onesc", [128, 1], BF16, kind="ExternalInput")
    onesr_in = nc.dram_tensor("onesr", [1, 128], F32, kind="ExternalInput")
    y_out = nc.dram_tensor("y", [NSEQ, S, D], F32, kind="ExternalOutput")

    wp_bf = nc.dram_tensor("wp_bf", [3 * D, D], BF16, kind="Internal")
    wo_bf = nc.dram_tensor("wo_bf", [D, D], BF16, kind="Internal")
    wg_bf = nc.dram_tensor("wg_bf", [IFF, D], BF16, kind="Internal")
    wu_bf = nc.dram_tensor("wu_bf", [IFF, D], BF16, kind="Internal")
    wd_bf = nc.dram_tensor("wd_bf", [D, IFF], BF16, kind="Internal")
    cwp = nc.dram_tensor("cwp", [DC, 128, 3 * D], BF16, kind="Internal")
    cwo = nc.dram_tensor("cwo", [DC, 128, D], BF16, kind="Internal")
    cwg = nc.dram_tensor("cwg", [DC, 128, IFF], BF16, kind="Internal")
    cwu = nc.dram_tensor("cwu", [DC, 128, IFF], BF16, kind="Internal")
    cwd = nc.dram_tensor("cwd", [ICN, 128, D], BF16, kind="Internal")
    xnd = nc.dram_tensor("xnd", [S, D], BF16, kind="Internal")
    vd = nc.dram_tensor("vd", [DC, 128, HD], BF16, kind="Internal")
    aod = nc.dram_tensor("aod", [NH, HD, S], BF16, kind="Internal")
    gd = nc.dram_tensor("gd", [S, D], BF16, kind="Internal")

    taps = {}
    for t in TAPS:
        if t == "lnstats":
            taps[t] = nc.dram_tensor("tap_lnstats", [NT128, 4, 128], F32,
                                     kind="ExternalOutput")
            continue
        shp = {"xn": [S, D], "g": [S, D], "ao": [NH, HD, S]}[t.rstrip("01")]
        taps[t] = nc.dram_tensor("tap_" + t, shp, BF16, kind="ExternalOutput")

    A0 = 18432
    my_sbuf = set()

    def sb(name, shape, dtype, off):
        my_sbuf.add(name)
        return nc.alloc_sbuf_tensor_at(name, shape, dtype, offset=off)

    xnT = sb("xnT", [128, DC * S], BF16, A0)
    h_all = sb("h_all", [128, ICN * 512], BF16, A0 + 65536)
    st = A0 + 131072
    gs_sb = sb("gs_sb", [128, ICN], F32, st); st += 256
    us_sb = sb("us_sb", [128, ICN], F32, st); st += 256
    mask_sb = sb("mask_sb", [128, 128], F32, st); st += 512
    dsb_sb = sb("dsb_sb", [128, D], BF16, st); st += 4096
    onesc_sb = sb("onesc_sb", [128, 1], BF16, st); st += 32
    onesr_sb = sb("onesr_sb", [1, 128], F32, st); st += 512
    sm_t = [sb(f"sm_t{i}", [128, 1], F32, st + 32 * i) for i in range(2)]; st += 64
    nmean_t = [sb(f"nmean_t{i}", [128, 1], F32, st + 32 * i) for i in range(2)]; st += 64
    ssq_t = [sb(f"ssq_t{i}", [128, 1], F32, st + 32 * i) for i in range(2)]; st += 64
    var_t = [sb(f"var_t{i}", [128, 1], F32, st + 32 * i) for i in range(2)]; st += 64
    lnv_t = [sb(f"lnv_t{i}", [128, 1], F32, st + 32 * i) for i in range(2)]; st += 64
    rstd_t = [sb(f"rstd_t{i}", [128, 1], F32, st + 32 * i) for i in range(2)]; st += 64
    smsc_sb = sb("smsc_sb", [128, 1], F32, st); st += 32
    st = (st + 31) & ~31

    S1 = st
    xa_tile = [sb(f"xa_tile{i}", [128, D], F32, S1 + 8192 * i) for i in range(2)]
    xn_tile = [sb(f"xn_tile{i}", [128, D], BF16, S1 + 16384 + 4096 * i) for i in range(2)]
    q_sb = sb("q_sb", [128, S], BF16, S1)
    k_sb = sb("k_sb", [128, S], BF16, S1 + 4096)
    vtm_sb = sb("vtm_sb", [128, DC * HD], BF16, S1 + 8192)
    v_sb = sb("v_sb", [128, S], BF16, S1 + 12288)
    attT_sb = sb("attT_sb", [128, 4096], BF16, S1 + 16384)
    wq_st = sb("wq_st", [128, DC * 128], BF16, S1 + 24576)
    wk_st = sb("wk_st", [128, DC * 128], BF16, S1 + 28672)
    wv_st = sb("wv_st", [128, DC * 128], BF16, S1 + 32768)
    ao_st0 = sb("ao_st0", [128, 512], BF16, S1 + 36864)
    ao_st1 = sb("ao_st1", [128, 512], BF16, S1 + 37888)
    S1_END = S1 + 38912

    S2 = S1_END
    zr_sb = sb("zr_sb", [1, 512], F32, S2)
    zbc_sb = sb("zbc_sb", [128, 512], F32, S2 + 2048)
    bt0 = sb("bt0", [128, 4096], BF16, S2 + 4096)
    bt1 = sb("bt1", [128, 4096], BF16, S2 + 12288)
    wg_st = [sb(f"wg_st{i}", [128, DC * 128], BF16, S2 + 4096 * i) for i in range(2)]
    wu_st = [sb(f"wu_st{i}", [128, DC * 128], BF16, S2 + 8192 + 4096 * i) for i in range(2)]
    wd_st = [sb(f"wd_st{i}", [128, 2048], BF16, S2 + 16384 + 4096 * i) for i in range(2)]
    silu_tmp = sb("silu_tmp", [128, 512], BF16, S2 + 24576)
    dtmp = sb("dtmp", [128, 512], F32, S2 + 25600)
    g_st = [sb(f"g_st{i}", [128, 1024], BF16, S2 + 27648 + 2048 * i) for i in range(2)]
    woq_st = sb("woq_st", [128, DC * 512], BF16, S2)
    ao_ld = [sb(f"ao_ld{i}", [128, DC * 128], BF16, S2 + 16384 + 4096 * i) for i in range(2)]
    xc_ld = [sb(f"xc_ld{i}", [128, 512], F32, S2 + 24576 + 2048 * i) for i in range(2)]
    gc_ld = [sb(f"gc_ld{i}", [128, 512], BF16, S2 + 28672 + 1024 * i) for i in range(2)]
    cmb = [sb(f"cmb{i}", [128, 512], F32, S2 + 30720 + 2048 * i) for i in range(2)]
    S2_END = S2 + 34816
    assert S2_END <= 229344, f"SBUF overflow: {S2_END}"

    es = ExitStack()
    with es:
        psumA = es.enter_context(nc.psum_tensor("psumA", [128, 2048], F32))
        psumB = es.enter_context(nc.psum_tensor("psumB", [128, 2048], F32))
        b = Builder(nc, es)

        ct = {}
        ct["gs"] = b.dma("sync", gs_sb[:], gs_in.ap())
        ct["us"] = b.dma("sync", us_sb[:], us_in.ap())
        ct["mask"] = b.dma("sync", mask_sb[:], mask_in.ap())
        ct["smsc"] = b.dma("sync", smsc_sb[:], smsc_in.ap())
        ct["onesc"] = b.dma("sync", onesc_sb[:], onesc_in.ap())
        ct["onesr"] = b.dma("sync", onesr_sb[:], onesr_in.ap())
        ct["dsb"] = b.dma("gpsimd", dsb_sb[:], dsb_in.ap())

        cast_toks = {}
        for nm, src, dst, rows in (("wp", wp_in, wp_bf, 3 * D), ("wo", wo_in, wo_bf, D),
                                   ("wg", wg_in, wg_bf, IFF), ("wu", wu_in, wu_bf, IFF),
                                   ("wd", wd_in, wd_bf, D)):
            toks = []
            for ch in range(rows // 2048):
                toks.append(b.dma("gpsimd", dst[ch * 2048:(ch + 1) * 2048, :],
                                  src[ch * 2048:(ch + 1) * 2048, :]))
            cast_toks[nm] = toks

        bt_free = [0, 0]
        cache_toks = {"wp": [], "wo": [], "wg": [], "wu": [], "wd": []}

        def build_block(nm, src_bf, cache, j, hf, rows, minus128):
            bid = hf
            buf = (bt0, bt1)[bid]
            half = rows // 2
            if j == 0:
                b.waitt("sync", cast_toks[nm])
            b.waitt("sync", bt_free[bid])
            vt = b.dma("sync", buf[:, 0:half],
                       src_bf[hf * half:(hf + 1) * half, j * 128:(j + 1) * 128],
                       transpose=True)
            if minus128:
                b.waitt("vector", vt)
                vv = b.ts(buf[:, 0:half], buf[:, 0:half], -128.0, None,
                          ALU.add, ALU.bypass)
                b.wait("scalar", b.c["vector"], vv)
            else:
                b.waitt("scalar", vt)
            tok = b.dma("scalar", cache[j, :, hf * half:(hf + 1) * half], buf[:, 0:half])
            bt_free[bid] = tok
            cache_toks[nm].append(tok)

        def build_all(nm, src_bf, cache, nblk, rows, minus128):
            for j in range(nblk):
                for hf in range(2):
                    build_block(nm, src_bf, cache, j, hf, rows, minus128)

        ffn_build_jobs = []
        if set(RUN_PHASES) & {"B", "D", "C"}:
            for nm, src_bf, cache, nblk, rows, m128 in (
                    ("wg", wg_bf, cwg, DC, IFF, True),
                    ("wu", wu_bf, cwu, DC, IFF, True),
                    ("wd", wd_bf, cwd, ICN, D, True)):
                for j in range(nblk):
                    for hf in range(2):
                        ffn_build_jobs.append(
                            (lambda nm=nm, s_=src_bf, c_=cache, j=j, hf=hf,
                                    r=rows, m=m128: build_block(nm, s_, c_, j, hf, r, m)))

        def make_A(s, G, t_xnT_prev):
            x_s = x_in[s]
            st_ = {"v_sq": [0, 0], "v_ap": [0, 0], "stok": [0, 0], "all": [],
                   "fence": False}

            def tile_job(t):
                p = t % 2
                if not st_["fence"]:
                    b.wait("sync", b.c["tensor"], G.get("B_t", 0))
                    b.wait("sync", b.c["vector"], G.get("B_v", 0))
                    b.wait("sync", b.c["scalar"], G.get("B_s", 0))
                    st_["fence"] = True
                b.wait("sync", b.c["scalar"], st_["v_sq"][p])
                b.wait("sync", b.c["vector"], st_["v_ap"][p])
                b.waitt("sync", st_["stok"][p])
                vx = b.dma("sync", xa_tile[p][:], x_s[t * 128:(t + 1) * 128, :])
                b.waitt("vector", vx)
                v1 = b.red(sm_t[p][:], xa_tile[p][:])
                b.wait("vector", b.c["vector"], v1)
                v2 = b.ts(nmean_t[p][:], sm_t[p][:], -1.0 / D, None, ALU.mult, ALU.bypass)
                b.waitt("scalar", vx)
                b.wait("scalar", b.c["gpsimd"], 0)
                v_sq = b.act(xn_tile[p][:], xa_tile[p][:], AF.Square)
                st_["v_sq"][p] = v_sq
                b.wait("vector", b.c["scalar"], v_sq)
                v3 = b.red(ssq_t[p][:], xn_tile[p][:])
                b.wait("vector", b.c["vector"], v3)
                v4 = b.tt(sm_t[p][:], nmean_t[p][:], nmean_t[p][:], ALU.mult)
                b.wait("vector", b.c["vector"], v4)
                v5 = b.ts(var_t[p][:], ssq_t[p][:], 1.0 / D, LN_EPS, ALU.mult, ALU.add)
                b.wait("vector", b.c["vector"], v5)
                v6 = b.tt(var_t[p][:], var_t[p][:], sm_t[p][:], ALU.subtract)
                b.wait("scalar", b.c["vector"], v6)
                v7 = b.act(lnv_t[p][:], var_t[p][:], AF.Ln)
                b.wait("scalar", b.c["scalar"], v7)
                v8 = b.act(rstd_t[p][:], lnv_t[p][:], AF.Exp, scale=-0.5)
                b.wait("vector", b.c["scalar"], v8)
                v_ap = b.ts(xn_tile[p][:], xa_tile[p][:], nmean_t[p][:], rstd_t[p][:],
                            ALU.add, ALU.mult)
                st_["v_ap"][p] = v_ap
                b.wait("gpsimd", b.c["vector"], v_ap)
                if t == 0:
                    b.waitt("gpsimd", t_xnT_prev)
                tok = b.dma("gpsimd", xnd[t * 128:(t + 1) * 128, :], xn_tile[p][:])
                st_["stok"][p] = tok
                st_["all"].append(tok)

            def finalize():
                b.waitt("sync", st_["all"])
                b.wait("sync", b.c["tensor"], G.get("D_t", 0))
                t_xnT = []
                for dc in range(DC):
                    t_xnT.append(b.dma("sync", xnT[:, dc * S:(dc + 1) * S],
                                       xnd[:, dc * 128:(dc + 1) * 128], transpose=True))
                if f"xn{s}" in taps:
                    b.waitt("gpsimd", st_["all"])
                    b.dma("gpsimd", taps[f"xn{s}"].ap(), xnd.ap())
                return t_xnT

            return [lambda t=t: tile_job(t) for t in range(NT128)], finalize

        BST = {"t_vtm": 0, "t_ao": [0, 0]}

        def phase_B(s, G, t_xnT, t_aold_prev, background):
            v_wst = G.get("B_t", 0)
            t_vtm = BST["t_vtm"]
            t_ao = BST["t_ao"]
            v_aoev_slot = [G.get("C_v", 0)] * 2
            v_exp_slot = [G.get("B_s", 0)] * 3
            v_attT_slot = [G.get("C_t", 0)] * 8
            v_evac_slot = [G.get("C_v", 0)] * 2
            v_vcp = G.get("C_v", 0)
            v_exp_last = 0
            t_aod_all = []
            first_xnT = [True]
            pending = [None]
            QP = (psumA[:, 0:512], psumA[:, 512:1024])
            AV = (psumA[:, 1024:1536], psumA[:, 1536:2048])
            SC = (psumB[:, 0:512], psumB[:, 512:1024], psumB[:, 1024:1536])
            ZS = psumB[0:1, 1536:2048]
            ZB = psumB[:, 1536:2048]
            if s == 0:
                b.waitt("scalar", ct["smsc"])
                b.waitt("tensor", [ct["onesc"], ct["onesr"]])
                b.waitt("vector", ct["mask"])
                b.waitt("sync", cache_toks["wp"])
            b.wait("tensor", b.c["vector"], G.get("C_v", 0))

            def emit_finish(fin):
                qg, h, v_end = fin
                av = qg % 2
                b.wait("vector", b.c["tensor"], v_end)
                vrec = b.recip(zr_sb[:], ZS)
                b.wait("tensor", b.c["vector"], vrec)
                vbc = b.mm(ZB, onesr_sb[:], zr_sb[:], start=True, stop=True, inc=1)
                b.wait("vector", b.c["tensor"], vbc)
                vcp = b.copy(zbc_sb[:], ZB)
                ao_buf = (ao_st0, ao_st1)[av]
                b.wait("vector", b.c["vector"], vcp)
                b.waitt("vector", t_ao[av])
                v_aoev = b.tt(ao_buf[:], AV[av], zbc_sb[:], ALU.mult)
                b.wait("gpsimd", b.c["vector"], v_aoev)
                if h == 0 and qg == 0:
                    b.waitt("gpsimd", t_aold_prev)
                t_ao[av] = b.dma("gpsimd", aod[h, :, qg * 512:(qg + 1) * 512],
                                 ao_buf[:])
                t_aod_all.append(t_ao[av])
                return v_aoev, vcp

            for h in range(NH):
                b.wait("sync", b.c["tensor"], v_wst)
                t_wld = []
                for st_buf, part in ((wq_st, 0), (wk_st, 1), (wv_st, 2)):
                    off = part * D + h * HD
                    t_wld.append(b.dma(
                        "sync", st_buf[:].rearrange("p (c i) -> p c i", c=DC),
                        cwp[:, :, off:off + HD].rearrange("c p i -> p c i")))

                def proj(st_buf, evac_to):
                    b.waitt("tensor", t_wld)
                    if first_xnT[0]:
                        b.waitt("tensor", t_xnT)
                        first_xnT[0] = False
                    last_evac = 0
                    vmm = 0
                    for sl in range(4):
                        qp = QP[sl % 2]
                        b.wait("tensor", b.c["vector"], v_evac_slot[sl % 2])
                        for dc in range(DC):
                            vmm = b.mm(qp, st_buf[:, dc * 128:(dc + 1) * 128],
                                       xnT[:, dc * S + sl * 512:dc * S + (sl + 1) * 512],
                                       start=(dc == 0), stop=(dc == DC - 1),
                                       inc=(1 if dc == DC - 1 else 0))
                        b.wait("vector", b.c["tensor"], vmm)
                        v_evac_slot[sl % 2] = b.copy(
                            evac_to[:, sl * 512:(sl + 1) * 512], qp)
                        last_evac = v_evac_slot[sl % 2]
                    return last_evac, vmm

                v_q_evac, _ = proj(wq_st, q_sb)
                v_k_evac, _ = proj(wk_st, k_sb)
                v_v_evac, vv_mm = proj(wv_st, v_sb)
                v_wst = vv_mm
                b.wait("gpsimd", b.c["vector"], v_v_evac)
                b.waitt("gpsimd", t_vtm)
                t_vst = b.dma("gpsimd", vd.rearrange("c h t -> h c t"),
                              v_sb[:].rearrange("p (c t) -> p c t", c=DC))
                b.waitt("sync", t_vst)
                b.wait("sync", b.c["tensor"], v_attT_slot[0])
                t_vtm = b.dma("sync", vtm_sb[:], vd.rearrange("c h t -> (c h) t"),
                              transpose=True)

                for qg in range(4):
                    if background:
                        for _ in range(3):
                            if ffn_build_jobs:
                                ffn_build_jobs.pop(0)()
                    nch = 4 * (qg + 1)
                    v_exp_chunk = [0] * nch
                    first_z = [True]

                    def emit_scores(c):
                        qlo = max(0, c * 128 - qg * 512)
                        sc = SC[c % 3]
                        b.wait("tensor", b.c["scalar"], v_exp_slot[c % 3])
                        if qg == 0 and c == 0:
                            b.wait("tensor", b.c["vector"], v_v_evac)
                        vsc = b.mm(sc[:, qlo:512], k_sb[:, c * 128:(c + 1) * 128],
                                   q_sb[:, qg * 512 + qlo:(qg + 1) * 512],
                                   start=True, stop=True, inc=1)
                        diag = c - 4 * qg
                        if 0 <= diag <= 3:
                            b.wait("vector", b.c["tensor"], vsc)
                            vmk = b.tt(sc[:, qlo:qlo + 128], sc[:, qlo:qlo + 128],
                                       mask_sb[:], ALU.add)
                            b.wait("scalar", b.c["vector"], vmk)
                        else:
                            b.wait("scalar", b.c["tensor"], vsc)
                        b.wait("scalar", b.c["tensor"], v_attT_slot[c % 8])
                        v_exp_chunk[c] = b.act(
                            attT_sb[:, (c % 8) * 512 + qlo:(c % 8) * 512 + 512],
                            sc[:, qlo:512], AF.Exp, scale=smsc_sb[:])
                        v_exp_slot[c % 3] = v_exp_chunk[c]

                    def emit_zattv(c):
                        qlo = max(0, c * 128 - qg * 512)
                        b.wait("tensor", b.c["scalar"], v_exp_chunk[c])
                        if first_z[0]:
                            b.wait("tensor", b.c["vector"], v_vcp)
                            b.wait("tensor", b.c["vector"], v_aoev_slot[qg % 2])
                            if qg == 0:
                                b.waitt("tensor", t_vtm)
                            first_z[0] = False
                        rhs = attT_sb[:, (c % 8) * 512 + qlo:(c % 8) * 512 + 512]
                        b.mm(ZS[:, qlo:512], onesc_sb[:], rhs,
                             start=(c == 0), stop=(c == nch - 1))
                        v_attT_slot[c % 8] = b.mm(
                            AV[qg % 2][:, qlo:512], vtm_sb[:, c * 128:(c + 1) * 128],
                            rhs, start=(c == 0), stop=(c == nch - 1), inc=1)

                    for c in range(nch + 2):
                        if c < nch:
                            emit_scores(c)
                        if c == 2 and pending[0] is not None:
                            v_ae, v_vcp = emit_finish(pending[0])
                            v_aoev_slot[pending[0][0] % 2] = v_ae
                            pending[0] = None
                        if c >= 2:
                            emit_zattv(c - 2)
                    v_exp_last = max(v_exp_slot)
                    pending[0] = (qg, h, b.c["tensor"].n)
            if pending[0] is not None:
                v_ae, v_vcp = emit_finish(pending[0])
                pending[0] = None
            BST["t_vtm"] = t_vtm
            BST["t_ao"] = t_ao
            if f"ao{s}" in taps:
                b.waitt("gpsimd", t_aod_all)
                b.dma("gpsimd", taps[f"ao{s}"].ap(), aod.ap())
            return t_aod_all

        def phase_D(s, G, t_gc_prev):
            t_gstore = [0, 0]
            v_w_ld = [G.get("B_t", 0)] * 2
            v_wd_ld = [G.get("B_t", 0)] * 2
            v_sil_bid = [G.get("B_s", 0)] * 2
            v_h_bid = [G.get("B_v", 0)] * 2
            v_devac = G.get("B_v", 0)
            v_dsilu = G.get("C_s", 0)
            t_gd = {}
            b.wait("sync", b.c["vector"], G.get("B_v", 0))
            if s == 0:
                b.waitt("scalar", ct["gs"])
                b.waitt("vector", [ct["us"], ct["dsb"]])
                b.waitt("sync", cache_toks["wg"] + cache_toks["wu"] + cache_toks["wd"])
            for tc in range(TCN):
                t0 = tc * 512
                for ic in range(ICN):
                    bid = ic % 2
                    b.wait("sync", b.c["tensor"], v_w_ld[bid])
                    tg = b.dma("sync",
                               wg_st[bid][:].rearrange("p (c i) -> p c i", c=DC),
                               cwg[:, :, ic * 128:(ic + 1) * 128].rearrange("c p i -> p c i"))
                    tu = b.dma("sync",
                               wu_st[bid][:].rearrange("p (c i) -> p c i", c=DC),
                               cwu[:, :, ic * 128:(ic + 1) * 128].rearrange("c p i -> p c i"))
                    gsl = bid * 512
                    usl = 1024 + bid * 512
                    b.waitt("tensor", [tg, tu])
                    b.wait("tensor", b.c["scalar"], v_sil_bid[bid])
                    b.wait("tensor", b.c["vector"], v_h_bid[bid])
                    if ic < 2:
                        b.wait("tensor", b.c["vector"], v_devac)
                    v_gmm = v_umm = 0
                    for dc in range(DC):
                        v_gmm = b.mm(psumA[:, gsl:gsl + 512],
                                     wg_st[bid][:, dc * 128:(dc + 1) * 128],
                                     xnT[:, dc * S + t0:dc * S + t0 + 512],
                                     start=(dc == 0), stop=(dc == DC - 1),
                                     inc=(1 if dc == DC - 1 else 0))
                    for dc in range(DC):
                        v_umm = b.mm(psumA[:, usl:usl + 512],
                                     wu_st[bid][:, dc * 128:(dc + 1) * 128],
                                     xnT[:, dc * S + t0:dc * S + t0 + 512],
                                     start=(dc == 0), stop=(dc == DC - 1),
                                     inc=(1 if dc == DC - 1 else 0))
                    v_w_ld[bid] = v_umm
                    b.wait("scalar", b.c["tensor"], v_gmm)
                    b.wait("scalar", b.c["vector"], v_h_bid[1 - bid])
                    v_sil_bid[bid] = b.act(silu_tmp[:], psumA[:, gsl:gsl + 512],
                                           AF.Silu, scale=gs_sb[:, ic:ic + 1])
                    b.wait("vector", b.c["scalar"], v_sil_bid[bid])
                    b.wait("vector", b.c["tensor"], v_umm)
                    v_h_bid[bid] = b.stt(h_all[:, ic * 512:(ic + 1) * 512],
                                         psumA[:, usl:usl + 512],
                                         us_sb[:, ic:ic + 1], silu_tmp[:],
                                         ALU.mult, ALU.mult)
                v_h_all = v_h_bid[(ICN - 1) % 2]
                for hh in range(2):
                    hof = hh * 1024
                    for ic in range(ICN):
                        pair, sub = ic // 2, ic % 2
                        bid = pair % 2
                        if sub == 0:
                            b.wait("sync", b.c["tensor"], v_wd_ld[bid])
                            tw = b.dma("sync",
                                       wd_st[bid][:].rearrange("p (c h) -> p c h", c=2),
                                       cwd[ic:ic + 2, :, hof:hof + 1024].rearrange("c p h -> p c h"))
                            tw_cur = tw
                            b.waitt("tensor", tw_cur)
                        if ic == 0:
                            b.wait("tensor", b.c["vector"], v_h_all)
                            b.wait("tensor", b.c["vector"], v_devac)
                        for tl in range(4):
                            ps = (psumA, psumA, psumB, psumB)[tl]
                            po = (0, 1024, 0, 1024)[tl]
                            for n in range(2):
                                b.mm(ps[:, po + n * 512:po + (n + 1) * 512],
                                     h_all[:, ic * 512 + tl * 128:ic * 512 + (tl + 1) * 128],
                                     wd_st[bid][:, sub * 1024 + n * 512:sub * 1024 + (n + 1) * 512],
                                     start=(ic == 0), stop=(ic == ICN - 1),
                                     inc=(1 if (tl == 3 and n == 1) else 0))
                        if sub == 1:
                            v_wd_ld[bid] = b.c["tensor"].n
                    v_dmm_all = b.c["tensor"].n
                    for tl in range(4):
                        ps = (psumA, psumA, psumB, psumB)[tl]
                        po = (0, 1024, 0, 1024)[tl]
                        gb = tl % 2
                        for hf2 in range(2):
                            b.wait("vector", b.c["tensor"], v_dmm_all)
                            b.wait("vector", b.c["scalar"], v_dsilu)
                            v_devac = b.tt(dtmp[:], ps[:, po + hf2 * 512:po + (hf2 + 1) * 512],
                                           dsb_sb[:, hof + hf2 * 512:hof + (hf2 + 1) * 512],
                                           ALU.mult)
                            b.wait("scalar", b.c["vector"], v_devac)
                            if hf2 == 0:
                                b.waitt("scalar", t_gstore[gb])
                            v_dsilu = b.act(g_st[gb][:, hf2 * 512:(hf2 + 1) * 512],
                                            dtmp[:], AF.Silu)
                        b.wait("gpsimd", b.c["scalar"], v_dsilu)
                        tglob = tc * 4 + tl
                        if tglob < 2 and hh == 0:
                            b.waitt("gpsimd", t_gc_prev)
                            t_gc_prev = 0
                        tok = b.dma(
                            "gpsimd",
                            gd[t0 + tl * 128:t0 + (tl + 1) * 128, hof:hof + 1024],
                            g_st[gb][:])
                        t_gstore[gb] = tok
                        t_gd.setdefault(tglob, []).append(tok)
            if f"g{s}" in taps:
                b.waitt("gpsimd", [t for t in t_gstore if t])
                b.dma("gpsimd", taps[f"g{s}"].ap(), gd.ap())
            return t_gd

        def phase_C(s, G, t_gd, t_aod_all, a_jobs=None):
            t_x = [0, 0]
            t_g = [0, 0]
            t_store = [0, 0]
            v_cmb = [G.get("D_v", 0)] * 2
            v_mm = [G.get("D_t", 0)] * 2
            t_gc_all = []
            t_aold_all = []
            b.wait("sync", b.c["scalar"], G.get("D_s", 0))
            b.wait("sync", b.c["vector"], G.get("D_v", 0))
            b.waitt("sync", t_aod_all)
            b.waitt("sync", [tk for toks in t_gd.values() for tk in toks])
            if s == 0:
                b.waitt("sync", cache_toks["wo"])
            first_cmb = [True]
            for oq in range(4):
                oof = oq * 512
                b.wait("sync", b.c["tensor"], max(v_mm))
                t_wo = b.dma("sync", woq_st[:].rearrange("p (c o) -> p c o", c=DC),
                             cwo[:, :, oof:oof + 512].rearrange("c p o -> p c o"))
                for t in range(NT128):
                    if a_jobs and (oq * NT128 + t) % 4 == 1 and a_jobs:
                        a_jobs.pop(0)()
                    bid = t % 2
                    b.wait("sync", b.c["tensor"], v_mm[bid])
                    b.wait("sync", b.c["vector"], v_cmb[bid])
                    t_ao_ld = b.dma(
                        "sync", ao_ld[bid][:].rearrange("p (c i) -> p c i", c=DC),
                        aod[:, :, t * 128:(t + 1) * 128].rearrange("c p i -> p c i"))
                    t_aold_all.append(t_ao_ld)
                    t_x[bid] = b.dma("sync", xc_ld[bid][:],
                                     x_in[s][t * 128:(t + 1) * 128, oof:oof + 512])
                    t_g[bid] = b.dma("sync", gc_ld[bid][:],
                                     gd[t * 128:(t + 1) * 128, oof:oof + 512])
                    t_gc_all.append(t_g[bid])
                    ps = (psumA, psumB)[bid]
                    b.waitt("tensor", [t_wo, t_ao_ld])
                    b.wait("tensor", b.c["vector"], v_cmb[bid])
                    vm = 0
                    for dc in range(DC):
                        vm = b.mm(ps[:, 0:512], ao_ld[bid][:, dc * 128:(dc + 1) * 128],
                                  woq_st[:, dc * 512:(dc + 1) * 512],
                                  start=(dc == 0), stop=(dc == DC - 1),
                                  inc=(1 if dc == DC - 1 else 0))
                    v_mm[bid] = vm
                    b.wait("vector", b.c["tensor"], vm)
                    if first_cmb[0]:
                        b.wait("vector", b.c["scalar"], G.get("D_s", 0))
                        first_cmb[0] = False
                    b.waitt("vector", [t_x[bid], t_g[bid], t_store[bid]])
                    vc1 = b.stt(cmb[bid][:], ps[:, 0:512], 1.0, gc_ld[bid][:],
                                ALU.mult, ALU.mult)
                    b.wait("vector", b.c["vector"], vc1)
                    v_cmb[bid] = b.tt(cmb[bid][:], cmb[bid][:], xc_ld[bid][:], ALU.add)
                    b.wait("gpsimd", b.c["vector"], v_cmb[bid])
                    t_store[bid] = b.dma(
                        "gpsimd", y_out[s][t * 128:(t + 1) * 128, oof:oof + 512],
                        cmb[bid][:])
            return t_gc_all, t_aold_all

        G = {}
        t_xnT_prev = 0
        t_aold_prev = 0
        t_gc_prev = 0
        first = True
        pending_A = None
        t_xnT = []
        for idx, s in enumerate(RUN_SEQS):
            if "A" in RUN_PHASES:
                if pending_A is None:
                    jobs, fin = make_A(s, G, t_xnT_prev)
                    for j in jobs:
                        j()
                    t_xnT = fin()
                else:
                    jobs, fin = pending_A
                    for j in jobs:
                        j()
                    t_xnT = fin()
                t_xnT_prev = t_xnT
            if first and set(RUN_PHASES) & {"B", "C"}:
                build_all("wp", wp_bf, cwp, DC, 3 * D, False)
                build_all("wo", wo_bf, cwo, DC, D, False)
            if "B" in RUN_PHASES:
                t_aod_all = phase_B(s, G, t_xnT, t_aold_prev, background=first)
                if first:
                    while ffn_build_jobs:
                        ffn_build_jobs.pop(0)()
                G["B_t"], G["B_v"], G["B_s"] = (b.c["tensor"].n, b.c["vector"].n,
                                                b.c["scalar"].n)
            else:
                t_aod_all = []
            if "D" in RUN_PHASES:
                t_gd = phase_D(s, G, t_gc_prev)
                G["D_t"], G["D_v"], G["D_s"] = (b.c["tensor"].n, b.c["vector"].n,
                                                b.c["scalar"].n)
            else:
                t_gd = {}
            nxt = RUN_SEQS[idx + 1] if idx + 1 < len(RUN_SEQS) else None
            a_jobs = None
            if nxt is not None and "A" in RUN_PHASES:
                jobs, fin = make_A(nxt, dict(G), t_xnT_prev)
                pending_A = (jobs, fin)
                a_jobs = jobs
            if "C" in RUN_PHASES:
                t_gc_prev, t_aold_prev = phase_C(s, G, t_gd, t_aod_all, a_jobs)
                G["C_t"], G["C_v"], G["C_s"] = (b.c["tensor"].n, b.c["vector"].n,
                                                b.c["scalar"].n)
            first = False
        b.barrier()

        with nc.Block() as block:
            @block.sync
            def _(eng):
                _replay(eng, b.streams["sync"])

            @block.gpsimd
            def _(eng):
                _replay(eng, b.streams["gpsimd"])

            @block.vector
            def _(eng):
                _replay(eng, b.streams["vector"])

            @block.scalar
            def _(eng):
                _replay(eng, b.streams["scalar"])

            @block.tensor
            def _(eng):
                _replay(eng, b.streams["tensor"])

    for f in nc.m.functions:
        for alloc in f.allocations:
            locs = getattr(alloc, "memorylocations", None)
            if not locs:
                continue
            for loc in locs:
                if loc.type != "SB":
                    continue
                base = loc.name.split("_set")[0]
                parts = base.rsplit("_", 1)
                if len(parts) == 2 and parts[1].isdigit():
                    base = parts[0]
                if base in my_sbuf:
                    continue
                sz = 1
                for d_ in (loc.dims or [1]):
                    sz *= d_
                if loc.name.startswith("DynamicDMAScratch"):
                    continue
                assert loc.addr + sz <= A0 or loc.addr >= S2_END, \
                    f"auto SBUF alloc {loc.name} at {loc.addr} overlaps manual region"
    return nc


def kernel(L0_attn_x, L0_attn_w_project, L0_attn_w_out, L0_ffn_gate_w,
           L0_ffn_gate_scale, L0_ffn_up_w, L0_ffn_up_scale, L0_ffn_down_w,
           L0_ffn_down_scale):
    x = np.ascontiguousarray(np.asarray(L0_attn_x, dtype=np.float32))
    wp = np.ascontiguousarray(np.asarray(L0_attn_w_project, dtype=np.float32))
    wo = np.ascontiguousarray(np.asarray(L0_attn_w_out, dtype=np.float32))
    wg = np.ascontiguousarray(np.asarray(L0_ffn_gate_w, dtype=np.int32))
    wu = np.ascontiguousarray(np.asarray(L0_ffn_up_w, dtype=np.int32))
    wd = np.ascontiguousarray(np.asarray(L0_ffn_down_w, dtype=np.int32))
    gs = np.ascontiguousarray(np.asarray(L0_ffn_gate_scale, dtype=np.float32).reshape(ICN, 128).T)
    us = np.ascontiguousarray(np.asarray(L0_ffn_up_scale, dtype=np.float32).reshape(ICN, 128).T)
    ds = np.asarray(L0_ffn_down_scale, dtype=np.float32).reshape(1, -1)
    dsb = np.ascontiguousarray(np.broadcast_to(ds, (128, D)))
    mask = np.ascontiguousarray(np.tril(np.full((128, 128), -1e9, dtype=np.float32), k=-1))
    smsc = np.full((128, 1), SM_SCALE, dtype=np.float32)
    onesc = np.ones((128, 1), dtype=bfloat16)
    onesr = np.ones((1, 128), dtype=np.float32)

    nc = build_nc()
    in_maps = []
    for c in range(NCORE):
        in_maps.append({"x": x[NSEQ * c:NSEQ * (c + 1)], "wp": wp, "wo": wo,
                        "wg": wg, "wu": wu, "wd": wd, "gs": gs, "us": us,
                        "dsb": dsb, "mask": mask, "smsc": smsc,
                        "onesc": onesc, "onesr": onesr})
    res = run_bass_kernel_spmd(nc, in_maps, core_ids=list(range(NCORE)))
    out = np.concatenate([r["y"] for r in res.results], axis=0)
    if TAPS:
        kernel.tap_results = [{k: v for k, v in r.items() if k.startswith("tap_")}
                              for r in res.results]
    return out
